# version 16 (final)
# baseline (speedup 1.0000x reference)
import numpy as np
import ml_dtypes
import concourse.bass as bass
import concourse.mybir as mybir
from concourse.bass_utils import run_bass_kernel_spmd
from contextlib import ExitStack

F32 = mybir.dt.float32
BF16 = mybir.dt.bfloat16
U8 = mybir.dt.uint8
AF = mybir.ActivationFunctionType
ALU = mybir.AluOpType
AX = mybir.AxisListType
BFDT = ml_dtypes.bfloat16

D = 1024
H = 16
HD = 64
TLOC = 2080
NCH = 17
KC = 8
NST = 5
STW = 416
NPE = 20
NG = 5
NEG = -1.0e30

STAGE = "full"
DEBUG_TAPS = ()


class Sem:
    def __init__(self, nc, es, name):
        self.h = es.enter_context(nc.semaphore(name))
        self.n = 0

    def inc(self, instr, k=1):
        instr.then_inc(self.h, k)
        self.n += k
        _EVENTS.append(("last", "inc", self.h.name, k))
        return self.n


_WAITS = []
_EVENTS = []


class TRK:

    def __init__(self, eng):
        self._e = eng
        self.last = None

    def __getattr__(self, name):
        a = getattr(self._e, name)
        if callable(a):
            def call(*args, **kw):
                eng = getattr(self._e, "engine", "?")
                if name == "wait_ge" and len(args) >= 2:
                    import traceback
                    _WAITS.append((args[0], args[1],
                                   traceback.extract_stack()[-3:-1]))
                    _EVENTS.append((str(eng), "wait", args[0].name, args[1]))
                r = a(*args, **kw)
                if hasattr(r, "then_inc"):
                    self.last = r
                    _EVENTS.append((str(eng), "op", None, None))
                return r
            return call
        return a


def cm_layout(wmat):
    K, M = wmat.shape
    assert K % 128 == 0
    return np.ascontiguousarray(
        wmat.reshape(K // 128, 128, M).transpose(1, 0, 2).reshape(128, -1))


def build_program(stage=None, taps=None):
    stage = stage or STAGE
    taps = taps if taps is not None else DEBUG_TAPS
    nc = bass.Bass("TRN2", target_bir_lowering=False,
                   dynamic_dma_scratch_size=16384)

    x_l = nc.dram_tensor("x_l", [TLOC, D], F32, kind="ExternalInput").ap()
    wg_cat = nc.dram_tensor("wg_cat", [NPE * 128, 2048], F32, kind="ExternalInput").ap()
    wu_cat = nc.dram_tensor("wu_cat", [NPE * 128, 2048], F32, kind="ExternalInput").ap()
    wd_cat = nc.dram_tensor("wd_cat", [NG * 128, 8192], F32, kind="ExternalInput").ap()
    rw_cm = nc.dram_tensor("rw_cm", [128, 128], F32, kind="ExternalInput").ap()
    wq_cm = nc.dram_tensor("wq_cm", [2 * 128, 4096], F32, kind="ExternalInput").ap()
    wk_cm = nc.dram_tensor("wk_cm", [2 * 128, 4096], F32, kind="ExternalInput").ap()
    wv_cm = nc.dram_tensor("wv_cm", [2 * 128, 4096], F32, kind="ExternalInput").ap()
    wo_cm = nc.dram_tensor("wo_cm", [128, 8192], F32, kind="ExternalInput").ap()
    bq_pp = nc.dram_tensor("bq_pp", [128, 8], F32, kind="ExternalInput").ap()
    bk_pp = nc.dram_tensor("bk_pp", [128, 8], F32, kind="ExternalInput").ap()
    bvo_d = nc.dram_tensor("bvo_d", [2, D], F32, kind="ExternalInput").ap()
    masks_d = nc.dram_tensor("masks_d", [128, 480], BF16, kind="ExternalInput").ap()
    eyes_d = nc.dram_tensor("eyes_d", [128, 128], F32, kind="ExternalInput").ap()
    y_out = nc.dram_tensor("y_out", [2048, D], F32, kind="ExternalOutput").ap()

    tapd = {}
    for tname, shape, dt in (("ln1", [KC * 128, TLOC], BF16),
                             ("comb", [16, TLOC], BF16),
                             ("y1", [NCH * 128, D], F32),
                             ("hg", [KC * 128, TLOC], BF16),
                             ("wd", [128, 8192], BF16),
                             ("cb", [2 * 128, 2080], BF16),
                             ("z", [NCH * 128, D], F32)):
        if tname in taps:
            tapd[tname] = nc.dram_tensor("tap_" + tname, shape, dt,
                                         kind="ExternalOutput").ap()

    es = ExitStack()
    def sbt(name, shape, dtype):
        return es.enter_context(nc.sbuf_tensor(name, shape, dtype))
    def pst(name, shape, dtype=F32):
        return es.enter_context(nc.psum_tensor(name, shape, dtype))

    y = [sbt(f"y{t}", [128, D], F32) for t in range(NCH)]
    lnq = [sbt(f"lnq{j}", [128, 2080], BF16) for j in range(KC)]
    hv = sbt("hv", [128, 34816], U8)
    h_g = [hv.ap()[:, j * 4352:(j + 1) * 4352].bitcast(BF16) for j in range(KC)]
    v_tm = [hv.ap()[:, t * 2048:(t + 1) * 2048].bitcast(BF16) for t in range(NCH)]
    dstrm = sbt("dstrm", [128, 33536], U8)
    wd_bf = dstrm.ap()[:, 0:16384].bitcast(BF16)
    wgu_slot = [dstrm.ap()[:, 16384 + i * 4096:16384 + (i + 1) * 4096].bitcast(BF16)
                for i in range(4)]
    k_fm = [dstrm.ap()[:, j * 4192:(j + 1) * 4192].bitcast(BF16) for j in range(KC)]
    earena = sbt("earena", [128, 24704], U8)
    lg_all = earena.ap()[:, 0:1088].bitcast(F32)
    comb_tm = earena.ap()[:, 1088:2176].bitcast(F32)
    rstat2 = earena.ap()[:, 2176:2688].bitcast(F32)
    msk_all = earena.ap()[:, 2688:3776].bitcast(F32)
    exps_all = earena.ap()[:, 3776:4864].bitcast(F32)
    wqkv_slot = [earena.ap()[:, i * 8192:(i + 1) * 8192].bitcast(BF16) for i in range(3)]
    p_buf = [earena.ap()[:, h * 768:(h + 1) * 768].bitcast(BF16) for h in range(8)]
    wo_bf = earena.ap()[:, 6144:22528].bitcast(BF16)

    lnstage = sbt("lnstage", [128, D], BF16)
    pt_sb = sbt("pt_sb", [128, 3072], BF16)
    sqscr = pt_sb.ap().bitcast(F32)[:, 0:D]
    lnstage2 = pt_sb.ap()[:, 2048:3072]
    stats = sbt("stats", [128, 64], F32)
    eps_t = sbt("eps_t", [128, 1], F32)
    rw_bf = sbt("rw_bf", [128, 128], BF16)
    sg_st = earena.ap()[:, 21376:21376 + 2 * 4 * STW].bitcast(BF16)
    mask_t = sbt("mask_t", [128, 480], BF16)
    eye_b = sbt("eye_b", [128, 128], BF16)
    bias_bc = sbt("bias_bc", [128, D], F32)
    bq_t = sbt("bq_t", [128, 8], F32)
    bk_t = sbt("bk_t", [128, 8], F32)
    stage_s = earena.ap()[:, 22528:23808].bitcast(F32)
    astat = earena.ap()[:, 23808:24192].bitcast(F32)
    vals = sbt("vals", [128, 1024], BF16)

    pb = [pst(f"pb{i}", [128, 512], F32) for i in range(8)]
    pbb = [p.ap().bitcast(BF16) for p in pb]

    s_x = Sem(nc, es, "s_x")
    s_c = Sem(nc, es, "s_c")
    s_cg = Sem(nc, es, "s_cg")
    s_pe = Sem(nc, es, "s_pe")
    s_dv = Sem(nc, es, "s_dv")
    s_ac = Sem(nc, es, "s_ac")
    s_st = Sem(nc, es, "s_st")
    s_wd = Sem(nc, es, "s_wd")
    s_wg = [Sem(nc, es, f"s_wg{i}") for i in range(2)]
    s_wq = [Sem(nc, es, f"s_wq{i}") for i in range(3)]
    s_wo = Sem(nc, es, "s_wo")
    s_rb = Sem(nc, es, "s_rb")

    block = es.enter_context(nc.Block())

    x_done = [0] * NCH

    @block.sync
    def _(sync):
        sync = TRK(sync)
        for t in range(NCH):
            r0, r1 = t * 128, min((t + 1) * 128, TLOC)
            s_x.inc(sync.dma_start(out=y[t][0:r1 - r0, :], in_=x_l[r0:r1, :]), 16)
        for t in range(NCH):
            x_done[t] = s_x.n
        s_c.inc(sync.dma_start(out=mask_t[:, :], in_=masks_d), 16)
        s_c.inc(sync.dma_start(out=bq_t[:, :], in_=bq_pp), 16)
        s_c.inc(sync.dma_start(out=bk_t[:, :], in_=bk_pp), 16)

    @block.gpsimd
    def _(g):
        g = TRK(g)
        s_cg.inc(g.dma_start(out=eye_b[:, :], in_=eyes_d[:, 0:128]), 16)
        s_cg.inc(g.dma_start(out=rw_bf[:, :], in_=rw_cm), 16)
    consts_done = s_c.n
    cg_done = s_cg.n

    @block.vector
    def _(v):
        v = TRK(v)
        v.memset(eps_t[:, :], 1e-5)
        v.memset(rstat2[:, :], 0.0)
        v.memset(lg_all[:, :], 0.0)
        v.wait_ge(s_x.h, x_done[NCH - 1])
        v.memset(y[NCH - 1][32:64, :], 0.0)
        v.memset(y[NCH - 1][64:128, :], 0.0)
        v.drain()
        s_dv.inc(v.last)
    junk_done = s_dv.n

    def emit_layernorm(src, dest_tiles, src_gate, dest_gate, bank_gate,
                       sq_pe_gate=0):
        @block.vector
        def _(v):
            v = TRK(v)
            for t in range(NCH):
                for gs, gv in (src_gate(t) or ()):
                    v.wait_ge(gs.h, gv)
                v.reduce_sum(stats[:, t:t + 1], src[t][:, :], axis=AX.X)
            v.drain()
            s_dv.inc(v.last)
        sums_done = s_dv.n

        @block.scalar
        def _(sc):
            sc = TRK(sc)
            if sq_pe_gate:
                sc.wait_ge(s_pe.h, sq_pe_gate)
            for gs, gv in (src_gate(7) or ()):
                sc.wait_ge(gs.h, gv)
            for t in range(8):
                sc.activation(sqscr[:, :], src[t][:, :], AF.Square,
                              accum_out=stats[:, 17 + t:18 + t])
                sc.drain()
            for gs, gv in (src_gate(NCH - 1) or ()):
                sc.wait_ge(gs.h, gv)
            for t in range(8, NCH):
                sc.activation(sqscr[:, :], src[t][:, :], AF.Square,
                              accum_out=stats[:, 17 + t:18 + t])
                sc.drain()
            s_ac.inc(sc.last)
        sqs_done = s_ac.n

        @block.vector
        def _(v):
            v = TRK(v)
            v.wait_ge(s_dv.h, sums_done)
            v.wait_ge(s_ac.h, sqs_done)
            v.wait_ge(s_c.h, consts_done)
            v.wait_ge(s_cg.h, cg_done)
            m = stats[:, 34:51]; q2 = stats[:, 17:34]; s1 = stats[:, 0:17]
            v.tensor_scalar_mul(m, s1, 1.0 / D)
            v.tensor_scalar_mul(q2, q2, 1.0 / D)
            v.drain()
            v.tensor_tensor(s1, m, m, ALU.mult)
            v.drain()
            v.tensor_tensor(q2, q2, s1, ALU.subtract)
            v.drain()
            s_dv.inc(v.last)
        var_done = s_dv.n

        @block.scalar
        def _(sc):
            sc = TRK(sc)
            sc.wait_ge(s_dv.h, var_done)
            sc.activation(stats[:, 0:17], stats[:, 17:34], AF.Ln, bias=eps_t[:, 0:1])
            sc.drain()
            sc.activation(stats[:, 17:34], stats[:, 0:17], AF.Exp, scale=-0.5)
            sc.drain()
            s_ac.inc(sc.last)
        rstd_done = s_ac.n

        @block.vector
        def _(v):
            v = TRK(v)
            v.wait_ge(s_ac.h, rstd_done)
            m = stats[:, 34:51]
            v.tensor_tensor(m, m, stats[:, 17:34], ALU.mult)
            v.drain()
            v.tensor_scalar_mul(m, m, -1.0)
            v.drain()
            s_dv.inc(v.last)
        ab_done = s_dv.n

        lnT = [0] * NCH
        lnC = [0] * NCH
        app = [0] * NCH
        for t in range(NCH):
            pg = t % 2
            lnst = lnstage if t % 2 == 0 else lnstage2

            @block.scalar
            def _(sc, t=t, pg=pg, lnst=lnst):
                sc = TRK(sc)
                sc.wait_ge(s_dv.h, ab_done)
                if t >= 2:
                    sc.wait_ge(s_pe.h, lnT[t - 2])
                sc.activation(lnst[:, :], src[t][:, :], AF.Identity,
                              bias=stats[:, 34 + t:35 + t],
                              scale=stats[:, 17 + t:18 + t])
                sc.drain()
                s_ac.inc(sc.last)
            app[t] = s_ac.n

            @block.tensor
            def _(te, t=t, pg=pg, lnst=lnst):
                te = TRK(te)
                te.wait_ge(s_ac.h, app[t])
                if t < 2:
                    if bank_gate:
                        te.wait_ge(s_dv.h, bank_gate)
                else:
                    te.wait_ge(s_dv.h, lnC[t - 2])
                for j in range(KC):
                    te.transpose(pbb[pg][:, j * 128:(j + 1) * 128],
                                 lnst[:, j * 128:(j + 1) * 128],
                                 eye_b[:, :])
                te.drain()
                s_pe.inc(te.last)
            lnT[t] = s_pe.n

            @block.vector
            def _(v, t=t, pg=pg):
                v = TRK(v)
                v.wait_ge(s_pe.h, lnT[t])
                if t == 0 and dest_gate:
                    v.wait_ge(s_pe.h, dest_gate)
                w = min(128, TLOC - t * 128)
                for j in range(KC):
                    v.tensor_copy(dest_tiles[j][:, t * 128:t * 128 + w],
                                  pbb[pg][:, j * 128:j * 128 + w])
                v.drain()
                s_dv.inc(v.last)
            lnC[t] = s_dv.n
        return lnC

    ln1_done = emit_layernorm(y, lnq,
                              src_gate=lambda t: ((s_x, x_done[t]), (s_dv, junk_done)),
                              dest_gate=0, bank_gate=0)
    ln1_all = ln1_done[NCH - 1]

    if "ln1" in tapd:
        @block.sync
        def _(sync):
            sync = TRK(sync)
            sync.wait_ge(s_dv.h, ln1_all)
            for j in range(KC):
                s_st.inc(sync.dma_start(out=tapd["ln1"][j * 128:(j + 1) * 128, :],
                                        in_=lnq[j][:, 0:TLOC]), 16)

    def emit_router(ln_done, entry_dv, entry_pe=0):
        @block.tensor
        def _(te):
            te = TRK(te)
            te.wait_ge(s_cg.h, cg_done)
            te.wait_ge(s_dv.h, ln_done[NCH - 1])
            if entry_dv:
                te.wait_ge(s_dv.h, entry_dv)
            for t in range(NCH):
                cw = min(128, TLOC - t * 128)
                for k in range(KC):
                    te.matmul(pb[6][0:cw, t * 16:t * 16 + 16],
                              lnq[k][:, t * 128:t * 128 + cw],
                              rw_bf[:, k * 16:(k + 1) * 16],
                              start=(k == 0), stop=(k == KC - 1))
            te.drain()
            s_pe.inc(te.last)
        lg_done = s_pe.n

        @block.vector
        def _(v):
            v = TRK(v)
            v.wait_ge(s_pe.h, lg_done)
            if entry_pe:
                v.wait_ge(s_pe.h, entry_pe)
            v.tensor_copy(lg_all[:, 0:256], pb[6][:, 0:256])
            v.tensor_copy(lg_all[0:32, 256:272], pb[6][0:32, 256:272])
            v.drain()
            s_dv.inc(v.last)
        lgE = [s_dv.n] * NCH

        Zc = rstat2[:, 0:17]
        e2c = rstat2[:, 17:34]
        denc = rstat2[:, 34:51]
        w1c = rstat2[:, 51:68]
        w2c = rstat2[:, 68:85]
        nm1c = rstat2[:, 85:102]
        nm2c = rstat2[:, 102:119]
        cwt = [min(128, TLOC - t * 128) for t in range(NCH)]

        def Lt(t):
            return lg_all[0:cwt[t], t * 16:(t + 1) * 16]

        def Ct(t):
            return comb_tm[0:cwt[t], t * 16:(t + 1) * 16]

        def Mt(t):
            return msk_all[0:cwt[t], t * 16:(t + 1) * 16]

        def chain():
            @block.vector
            def _(v):
                v = TRK(v)
                v.wait_ge(s_dv.h, lgE[NCH - 1])
                for t in range(NCH):
                    v.reduce_max(nm1c[0:cwt[t], t:t + 1], Lt(t), axis=AX.X,
                                 negate=True)
                v.drain()
                for t in range(NCH):
                    v.tensor_scalar(Ct(t), Lt(t), nm1c[0:cwt[t], t:t + 1], 0.0,
                                    ALU.add, ALU.is_ge)
                v.drain()
                for t in range(NCH):
                    v.scalar_tensor_tensor(Mt(t), Ct(t), NEG, Lt(t),
                                           ALU.mult, ALU.add)
                v.drain()
                for t in range(NCH):
                    v.reduce_max(nm2c[0:cwt[t], t:t + 1], Mt(t), axis=AX.X,
                                 negate=True)
                v.drain()
                s_dv.inc(v.last)
            mx_done = s_dv.n

            @block.scalar
            def _(sc):
                sc = TRK(sc)
                sc.wait_ge(s_dv.h, mx_done)
                for t in range(NCH):
                    sc.activation(exps_all[0:cwt[t], t * 16:(t + 1) * 16], Lt(t),
                                  AF.Exp, bias=nm1c[0:cwt[t], t:t + 1])
                for t in range(NCH):
                    sc.activation(e2c[0:cwt[t], t:t + 1],
                                  nm2c[0:cwt[t], t:t + 1], AF.Exp,
                                  bias=nm1c[0:cwt[t], t:t + 1], scale=-1.0)
                sc.drain()
                s_ac.inc(sc.last)
            ez_done = s_ac.n

            @block.vector
            def _(v):
                v = TRK(v)
                v.wait_ge(s_ac.h, ez_done)
                for t in range(NCH):
                    v.reduce_sum(Zc[0:cwt[t], t:t + 1],
                                 exps_all[0:cwt[t], t * 16:(t + 1) * 16],
                                 axis=AX.X)
                v.drain()
                v.tensor_scalar(denc, Zc, 1e-9, 1.0, ALU.mult, ALU.add)
                v.drain()
                v.tensor_tensor(denc, denc, e2c, ALU.add)
                v.drain()
                v.reciprocal(w1c, denc)
                v.drain()
                v.tensor_scalar_mul(w1c, w1c, 0.5)
                v.drain()
                v.tensor_tensor(w2c, e2c, w1c, ALU.mult)
                v.drain()
                for t in range(NCH):
                    v.tensor_scalar_mul(Ct(t), Ct(t), w1c[0:cwt[t], t:t + 1])
                for t in range(NCH):
                    v.tensor_scalar(Mt(t), Mt(t), nm2c[0:cwt[t], t:t + 1], 0.0,
                                    ALU.add, ALU.is_ge)
                v.drain()
                for t in range(NCH):
                    v.tensor_scalar_mul(Mt(t), Mt(t), w2c[0:cwt[t], t:t + 1])
                v.drain()
                for t in range(NCH):
                    v.tensor_tensor(Ct(t), Ct(t), Mt(t), ALU.add)
                v.drain()
                s_dv.inc(v.last)
            return s_dv.n

        return lgE[NCH - 1], chain

    if stage != "ln":
        lg1_last, chain1 = emit_router(ln1_done, 0)
    else:
        lg1_last, chain1 = 0, (lambda: 0)

    def emit_ffn(call, ln_all, entry_pe, entry_dv0, entry_dv1, chain_fn):
        entry_dv = entry_dv1
        gu_done = {}
        gu_st = {}
        hst = {}
        h_done = {}
        wload = {}
        evac = {}
        last_slot_user = {0: 0, 1: 0}
        prev_bank_user = {0: entry_dv0, 1: entry_dv1}
        dn_last = [entry_pe]
        comb_ready = [0]

        def emit_wgu(e):
            sl = e % 2

            @block.gpsimd
            def _(g, e=e, sl=sl):
                g = TRK(g)
                if e >= 2:
                    g.wait_ge(s_pe.h, gu_done[e - 2])
                elif entry_pe:
                    g.wait_ge(s_pe.h, entry_pe)
                s_wg[sl].inc(g.dma_start(out=wgu_slot[sl][:, :],
                                         in_=wg_cat[e * 128:(e + 1) * 128, :]), 16)
                s_wg[sl].inc(g.dma_start(out=wgu_slot[2 + sl][:, :],
                                         in_=wu_cat[e * 128:(e + 1) * 128, :]), 16)
            wload[e] = s_wg[sl].n

        for G in range(NG):
            emit_wgu(G * 4)
            emit_wgu(G * 4 + 1)

            @block.gpsimd
            def _(g, G=G):
                g = TRK(g)
                g.wait_ge(s_pe.h, dn_last[0])
                s_wd.inc(g.dma_start(out=wd_bf[:, :],
                                     in_=wd_cat[G * 128:(G + 1) * 128, :]), 16)
            wd_ready = s_wd.n

            for i in range(4):
                e = G * 4 + i
                sl = e % 2
                if i >= 2:
                    emit_wgu(e)

                for st in range(NST):
                    par = (e * NST + st) % 2
                    bs = par * 4

                    @block.tensor
                    def _(te, e=e, st=st, bs=bs, sl=sl, par=par):
                        te = TRK(te)
                        if st == 0:
                            te.wait_ge(s_wg[sl].h, wload[e])
                            te.wait_ge(s_dv.h, ln_all)
                        g = prev_bank_user[par]
                        if g:
                            te.wait_ge(s_dv.h, g)
                        elif entry_dv:
                            te.wait_ge(s_dv.h, entry_dv)
                        for mc in range(2):
                            for k in range(KC):
                                te.matmul(pb[bs + mc][:, 0:STW],
                                          wgu_slot[sl][:, k * 256 + mc * 128:
                                                       k * 256 + (mc + 1) * 128],
                                          lnq[k][:, st * STW:(st + 1) * STW],
                                          start=(k == 0), stop=(k == KC - 1))
                        for mc in range(2):
                            for k in range(KC):
                                te.matmul(pb[bs + 2 + mc][:, 0:STW],
                                          wgu_slot[2 + sl][:, k * 256 + mc * 128:
                                                           k * 256 + (mc + 1) * 128],
                                          lnq[k][:, st * STW:(st + 1) * STW],
                                          start=(k == 0), stop=(k == KC - 1))
                        te.drain()
                        s_pe.inc(te.last)
                    gu_st[(e, st)] = s_pe.n

                    @block.scalar
                    def _(sc, e=e, st=st, bs=bs, par=par):
                        sc = TRK(sc)
                        sc.wait_ge(s_pe.h, gu_st[(e, st)])
                        if last_slot_user[par]:
                            sc.wait_ge(s_dv.h, last_slot_user[par])
                        for mc in range(2):
                            sc.activation(sg_st[:, (2 * par + mc) * STW:
                                                (2 * par + mc + 1) * STW],
                                          pb[bs + mc][:, 0:STW], AF.Silu)
                        sc.drain()
                        s_ac.inc(sc.last)
                    silu_done = s_ac.n

                    @block.vector
                    def _(v, e=e, st=st, bs=bs, par=par, i=i,
                          silu_done=silu_done):
                        v = TRK(v)
                        v.wait_ge(s_ac.h, silu_done)
                        for mc in range(2):
                            dst = h_g[2 * i + mc][:, st * STW:(st + 1) * STW]
                            v.tensor_tensor(dst,
                                            sg_st[:, (2 * par + mc) * STW:
                                                  (2 * par + mc + 1) * STW],
                                            pb[bs + 2 + mc][:, 0:STW], ALU.mult)
                        v.drain()
                        s_dv.inc(v.last)
                    hst[(e, st)] = s_dv.n
                    last_slot_user[par] = s_dv.n
                    prev_bank_user[par] = s_dv.n
                gu_done[e] = gu_st[(e, NST - 1)]
                h_done[e] = s_dv.n
                if e % (4 * NG) == 1 and chain_fn is not None:
                    comb_ready[0] = chain_fn()

            if G == 0:
                dn_mm = {}
                for t in range(NCH):
                    cw = min(128, TLOC - t * 128)
                    bk = 4 if t % 2 == 0 else 6

                    @block.tensor
                    def _(te, G=G, t=t, cw=cw, bk=bk):
                        te = TRK(te)
                        te.wait_ge(s_dv.h, h_done[G * 4 + 3])
                        te.wait_ge(s_wd.h, wd_ready)
                        if t >= 2:
                            te.wait_ge(s_dv.h, evac[(G, t - 2)])
                        elif entry_dv:
                            te.wait_ge(s_dv.h, entry_dv)
                        for hf in range(2):
                            for k in range(KC):
                                te.matmul(pb[bk + hf][0:cw, 0:512],
                                          h_g[k][:, t * 128:t * 128 + cw],
                                          wd_bf[:, k * 1024 + hf * 512:
                                                k * 1024 + (hf + 1) * 512],
                                          start=(k == 0), stop=(k == KC - 1))
                        te.drain()
                        s_pe.inc(te.last)
                    dn_mm[t] = s_pe.n

                    @block.vector
                    def _(v, G=G, t=t, cw=cw, bk=bk):
                        v = TRK(v)
                        v.wait_ge(s_pe.h, dn_mm[t])
                        for hf in range(2):
                            v.scalar_tensor_tensor(y[t][0:cw, hf * 512:(hf + 1) * 512],
                                                   pb[bk + hf][0:cw, 0:512], 0.5,
                                                   y[t][0:cw, hf * 512:(hf + 1) * 512],
                                                   ALU.mult, ALU.add)
                        v.drain()
                        s_dv.inc(v.last)
                    evac[(G, t)] = s_dv.n
                dn_last[0] = dn_mm[NCH - 1]
                dn_mm_last = dn_mm[NCH - 1]
                prev_bank_user[1] = max(prev_bank_user[1], evac[(G, NCH - 1)])
            else:
                seq = 0
                dn_u = {}
                for t in range(NCH):
                    cw = min(128, TLOC - t * 128)
                    for ei in range(4):
                        pair = 4 + 2 * (seq % 2)

                        @block.tensor
                        def _(te, G=G, t=t, cw=cw, ei=ei, pair=pair, seq=seq):
                            te = TRK(te)
                            if seq == 0:
                                te.wait_ge(s_dv.h, h_done[G * 4 + 3])
                                te.wait_ge(s_wd.h, wd_ready)
                            if seq >= 2:
                                te.wait_ge(s_dv.h, evac[(G, seq - 2)])
                            elif G == 1:
                                te.wait_ge(s_dv.h, evac[(0, NCH - 1)])
                            else:
                                te.wait_ge(s_dv.h, evac[(G - 1, 4 * NCH - 1)])
                            for hf in range(2):
                                for k2 in range(2):
                                    k = 2 * ei + k2
                                    te.matmul(pb[pair + hf][0:cw, 0:512],
                                              h_g[k][:, t * 128:t * 128 + cw],
                                              wd_bf[:, k * 1024 + hf * 512:
                                                    k * 1024 + (hf + 1) * 512],
                                              start=(k2 == 0), stop=(k2 == 1))
                            te.drain()
                            s_pe.inc(te.last)
                        dn_u[seq] = s_pe.n

                        @block.vector
                        def _(v, G=G, t=t, cw=cw, ei=ei, pair=pair, seq=seq):
                            v = TRK(v)
                            v.wait_ge(s_pe.h, dn_u[seq])
                            r = G * 4 + ei - 4
                            ccol = comb_tm[0:cw, t * 16 + r:t * 16 + r + 1]
                            for hf in range(2):
                                v.scalar_tensor_tensor(
                                    y[t][0:cw, hf * 512:(hf + 1) * 512],
                                    pb[pair + hf][0:cw, 0:512], ccol,
                                    y[t][0:cw, hf * 512:(hf + 1) * 512],
                                    ALU.mult, ALU.add)
                            v.drain()
                            s_dv.inc(v.last)
                        evac[(G, seq)] = s_dv.n
                        seq += 1
                dn_last[0] = dn_u[4 * NCH - 1]
                dn_mm_last = dn_u[4 * NCH - 1]
                prev_bank_user[1] = max(prev_bank_user[1], evac[(G, 4 * NCH - 1)])

        return ({t: evac[(NG - 1, 4 * t + 3)] for t in range(NCH)},
                gu_done[NG * 4 - 1], dn_mm_last, h_done[NG * 4 - 1])

    if stage != "ln":
        y1_done, ffn1_gu_last, ffn1_dn_last, ffn1_h_last = \
            emit_ffn(0, ln1_all, 0, 0, lg1_last, chain1)
    else:
        y1_done = {t: junk_done for t in range(NCH)}
        ffn1_gu_last = ffn1_dn_last = ffn1_h_last = 0

    if "wd" in tapd and stage != "ln":
        @block.sync
        def _(sync):
            sync = TRK(sync)
            sync.wait_ge(s_pe.h, ffn1_dn_last)
            s_st.inc(sync.dma_start(out=tapd["wd"], in_=wd_bf[:, :]), 16)

    if "hg" in tapd and stage != "ln":
        @block.sync
        def _(sync):
            sync = TRK(sync)
            sync.wait_ge(s_dv.h, ffn1_h_last)
            for j in range(KC):
                s_st.inc(sync.dma_start(out=tapd["hg"][j * 128:(j + 1) * 128, :],
                                        in_=h_g[j][:, 0:TLOC]), 16)

    if "y1" in tapd and stage != "ln":
        @block.sync
        def _(sync):
            sync = TRK(sync)
            for t in range(NCH):
                sync.wait_ge(s_dv.h, y1_done[t])
                s_st.inc(sync.dma_start(out=tapd["y1"][t * 128:(t + 1) * 128, :],
                                        in_=y[t][:, :]), 16)

    if stage in ("attn", "full"):
        ln2_done = emit_layernorm(y, lnq,
                                  src_gate=lambda t: ((s_dv, y1_done[t]),),
                                  dest_gate=ffn1_gu_last,
                                  bank_gate=y1_done[NCH - 1])
        ln2_all = ln2_done[NCH - 1]

        @block.vector
        def _(v):
            v = TRK(v)
            v.wait_ge(s_pe.h, ffn1_dn_last)
            for j in range(KC):
                v.memset(k_fm[j][:, 0:16], 0.0)
            v.drain()
            s_dv.inc(v.last)
        kz_done = s_dv.n

        part = {}

        def stream_part(name, src, half, slot, gate_pe, gate_dv=0):
            @block.gpsimd
            def _(g):
                g = TRK(g)
                if gate_pe:
                    g.wait_ge(s_pe.h, gate_pe)
                if gate_dv:
                    g.wait_ge(s_dv.h, gate_dv)
                s_wq[slot].inc(g.dma_start(out=wqkv_slot[slot][:, :],
                                           in_=src[half * 128:(half + 1) * 128, :]), 16)
            part[(name, half)] = (slot, s_wq[slot].n)

        stream_part("k", wk_cm, 0, 0, 0, y1_done[NCH - 1])
        stream_part("k", wk_cm, 1, 1, 0, 0)
        stream_part("v", wv_cm, 0, 2, 0, ffn1_h_last)

        kq_evac = {}
        last_mm = {}

        def emit_proj(name, dest, bias, extra_gate_dv):
            for st in range(NST):
                @block.tensor
                def _(te, name=name, st=st):
                    te = TRK(te)
                    for half in range(2):
                        slot_h, thr_h = part[(name, half)]
                        te.wait_ge(s_wq[slot_h].h, thr_h)
                    te.wait_ge(s_dv.h, ln2_all)
                    if extra_gate_dv:
                        te.wait_ge(s_dv.h, extra_gate_dv)
                    if st >= 1:
                        te.wait_ge(s_ac.h, kq_evac[(name, st - 1)])
                    for m in range(KC):
                        for k in range(KC):
                            slot = part[(name, k // 4)][0]
                            te.matmul(pb[m][:, 0:STW],
                                      wqkv_slot[slot][:, (k % 4) * 1024 + m * 128:
                                                      (k % 4) * 1024 + (m + 1) * 128],
                                      lnq[k][:, st * STW:(st + 1) * STW],
                                      start=(k == 0), stop=(k == KC - 1))
                    te.drain()
                    s_pe.inc(te.last)
                last_mm[name] = s_pe.n
                mm_lm = s_pe.n

                @block.scalar
                def _(sc, name=name, st=st, dest=dest, bias=bias):
                    sc = TRK(sc)
                    sc.wait_ge(s_pe.h, mm_lm)
                    if name == "k":
                        sc.wait_ge(s_dv.h, kz_done)
                    for m in range(KC):
                        if name == "k":
                            out = dest[m][:, 16 + st * STW:16 + (st + 1) * STW]
                        else:
                            out = dest[m][:, st * STW:(st + 1) * STW]
                        sc.activation(out, pb[m][:, 0:STW], AF.Identity,
                                      bias=bias[:, m:m + 1])
                    sc.drain()
                    s_ac.inc(sc.last)
                kq_evac[(name, st)] = s_ac.n

        emit_proj("k", k_fm, bk_t, y1_done[NCH - 1])
        k_proj_done_pe = last_mm["k"]
        k_evac_all = kq_evac[("k", NST - 1)]

        stream_part("v", wv_cm, 1, 0, k_proj_done_pe)

        @block.gpsimd
        def _(g):
            g = TRK(g)
            s_rb.inc(g.dma_start(out=bias_bc[:, :],
                                 in_=bvo_d[0:1, :].partition_broadcast(128)), 16)
        bv_ready = s_rb.n
        v_mm = {}
        v_evac = {}
        for t in range(NCH):
            cw = min(128, TLOC - t * 128)
            bank = 4 + 2 * (t % 2)

            @block.tensor
            def _(te, t=t, cw=cw, bank=bank):
                te = TRK(te)
                for half in range(2):
                    slot_h, thr_h = part[("v", half)]
                    te.wait_ge(s_wq[slot_h].h, thr_h)
                te.wait_ge(s_c.h, consts_done)
                te.wait_ge(s_cg.h, cg_done)
                te.wait_ge(s_dv.h, ln2_all)
                if t >= 2:
                    te.wait_ge(s_dv.h, v_evac[t - 2])
                else:
                    te.wait_ge(s_ac.h, k_evac_all)
                for hf in range(2):
                    for k in range(KC):
                        slot = part[("v", k // 4)][0]
                        te.matmul(pb[bank + hf][0:cw, 0:512],
                                  lnq[k][:, t * 128:t * 128 + cw],
                                  wqkv_slot[slot][:, (k % 4) * 1024 + hf * 512:
                                                  (k % 4) * 1024 + (hf + 1) * 512],
                                  start=(k == 0), stop=(k == KC - 1))
                te.drain()
                s_pe.inc(te.last)
            v_mm[t] = s_pe.n

            @block.vector
            def _(v, t=t, cw=cw, bank=bank):
                v = TRK(v)
                v.wait_ge(s_pe.h, v_mm[t])
                v.wait_ge(s_rb.h, bv_ready)
                if t < 2:
                    v.wait_ge(s_dv.h, ffn1_h_last)
                for hf in range(2):
                    v.scalar_tensor_tensor(v_tm[t][0:cw, hf * 512:(hf + 1) * 512],
                                           pb[bank + hf][0:cw, 0:512], 1.0,
                                           bias_bc[0:cw, hf * 512:(hf + 1) * 512],
                                           ALU.mult, ALU.add)
                v.drain()
                s_dv.inc(v.last)
            v_evac[t] = s_dv.n
        v_all = v_evac[NCH - 1]
        v_mm_last = v_mm[NCH - 1]

        stream_part("q", wq_cm, 0, 1, k_proj_done_pe)
        stream_part("q", wq_cm, 1, 2, v_mm_last)
        emit_proj("q", lnq, bq_t, v_all)
        q_evac_all = kq_evac[("q", NST - 1)]
        q_mm_last = last_mm["q"]

        @block.gpsimd
        def _(g):
            g = TRK(g)
            g.wait_ge(s_pe.h, q_mm_last)
            s_wo.inc(g.dma_start(out=wo_bf[:, :], in_=wo_cm), 16)
            g.wait_ge(s_dv.h, v_all)
            s_rb.inc(g.dma_start(out=bias_bc[:, :],
                                 in_=bvo_d[1:2, :].partition_broadcast(128)), 16)
        wo_ready = s_wo.n
        bo_ready = s_rb.n

        @block.vector
        def _(v):
            v = TRK(v)
            v.wait_ge(s_pe.h, q_mm_last)
            for h in range(8):
                v.memset(p_buf[h][:, :], 0.0)
            v.drain()
            s_dv.inc(v.last)
        pz_done = s_dv.n

        A = {}
        import os
        attn_tmax = int(os.environ.get("ATTN_TMAX", NCH))
        attn_sub = int(os.environ.get("ATTN_SUB", 7))

        def mask_off(t):
            return 0 if t == 0 else (320 if t == NCH - 1 else 160)

        def emit_qk(t):
            cw = min(128, TLOC - t * 128)
            kw = 160 if t < NCH - 1 else 48
            for hh in range(2):
                heads = list(range(hh * 8, hh * 8 + 8))

                @block.tensor
                def _(te, t=t, cw=cw, hh=hh, heads=heads, kw=kw):
                    te = TRK(te)
                    te.wait_ge(s_ac.h, q_evac_all)
                    te.wait_ge(s_dv.h, v_all)
                    te.wait_ge(s_dv.h, pz_done)
                    te.wait_ge(s_c.h, consts_done)
                    te.wait_ge(s_cg.h, cg_done)
                    prev = A.get((t - 1, hh, "ptcopy"))
                    if prev:
                        te.wait_ge(s_dv.h, prev)
                    for grp in (0, 1):
                        for h in heads:
                            if h % 2 != grp:
                                continue
                            hl = h % 8
                            bank = 3 * hh + hl // 3
                            off = (hl % 3) * 160
                            jt = h // 2
                            pbase = 64 * grp
                            te.matmul(pb[bank][0:cw, off:off + kw],
                                      lnq[jt][pbase:pbase + 64, t * 128:t * 128 + cw],
                                      k_fm[jt][pbase:pbase + 64, t * 128:t * 128 + kw],
                                      start=True, stop=True)
                        te.drain()
                    s_pe.inc(te.last)
                A[(t, hh, "qk")] = s_pe.n

        def emit_front(t, halves):
            cw = min(128, TLOC - t * 128)
            kw = 160 if t < NCH - 1 else 48
            for hh in halves:
                heads = list(range(hh * 8, hh * 8 + 8))

                @block.vector
                def _(v, t=t, cw=cw, hh=hh, heads=heads, kw=kw):
                    v = TRK(v)
                    v.wait_ge(s_pe.h, A[(t, hh, "qk")])
                    for h in heads:
                        hl = h % 8
                        S = pb[3 * hh + hl // 3][0:cw,
                                                 (hl % 3) * 160:(hl % 3) * 160 + kw]
                        v.scalar_tensor_tensor(
                            S, S, 0.125,
                            mask_t[0:cw, mask_off(t):mask_off(t) + kw],
                            ALU.mult, ALU.add)
                    v.drain()
                    s_dv.inc(v.last)
                A[(t, hh, "stg")] = s_dv.n

                @block.scalar
                def _(sc, t=t, cw=cw, hh=hh, heads=heads, kw=kw):
                    sc = TRK(sc)
                    sc.wait_ge(s_dv.h, A[(t, hh, "stg")])
                    ptprev = A.get((t, 0, "pt")) if hh == 1 else \
                        A.get((t - 1, 1, "pt"))
                    if ptprev:
                        sc.wait_ge(s_pe.h, ptprev)
                    for h in heads:
                        hl = h % 8
                        S = pb[3 * hh + hl // 3][0:cw,
                                                 (hl % 3) * 160:(hl % 3) * 160 + kw]
                        sc.activation(p_buf[hl][0:cw, 112:112 + kw], S,
                                      AF.Exp)
                    sc.drain()
                    s_ac.inc(sc.last)
                A[(t, hh, "exp_all")] = s_ac.n

        pend_wo_evac = [None]
        if attn_tmax > 0:
            emit_qk(0)
            emit_front(0, (0,))
        for t in range(attn_tmax):
            cw = min(128, TLOC - t * 128)
            pg = t % 2
            nm = astat[:, pg * 48:pg * 48 + 16]
            Zs = astat[:, pg * 48 + 16:pg * 48 + 32]
            rr = astat[:, pg * 48 + 32:pg * 48 + 48]
            pieces = [c for c in (t - 1, t, t + 1) if 0 <= c <= NCH - 1]

            kw = 160 if t < NCH - 1 else 48
            for hh in range(2):
                heads = list(range(hh * 8, hh * 8 + 8))
                if hh == 1:
                    emit_front(t, (1,))

                if attn_sub < 3:
                    A[(t, hh, "scaled")] = s_dv.n
                    A[(t, hh, "pt")] = s_pe.n
                    A[(t, hh, "ptcopy")] = s_dv.n
                    A[(t, hh, "pv")] = s_pe.n
                    A[(t, hh, "vals")] = s_dv.n
                    continue

                @block.vector
                def _(v, t=t, cw=cw, hh=hh, heads=heads, Zs=Zs, rr=rr, kw=kw):
                    v = TRK(v)
                    v.wait_ge(s_ac.h, A[(t, hh, "exp_all")])
                    for h in heads:
                        v.reduce_sum(Zs[0:cw, h:h + 1],
                                     p_buf[h % 8][0:cw, 112:112 + kw], axis=AX.X)
                    v.drain()
                    v.reciprocal(rr[0:cw, hh * 8:hh * 8 + 8],
                                 Zs[0:cw, hh * 8:hh * 8 + 8])
                    v.drain()
                    for h in heads:
                        v.tensor_scalar_mul(p_buf[h % 8][0:cw, 112:112 + kw],
                                            p_buf[h % 8][0:cw, 112:112 + kw],
                                            rr[0:cw, h:h + 1])
                    v.drain()
                    s_dv.inc(v.last)
                A[(t, hh, "scaled")] = s_dv.n
                if hh == 0 and pend_wo_evac[0] is not None:
                    pend_wo_evac[0]()
                    pend_wo_evac[0] = None
                if attn_sub < 4:
                    A[(t, hh, "pt")] = s_pe.n
                    A[(t, hh, "ptcopy")] = s_dv.n
                    A[(t, hh, "pv")] = s_pe.n
                    A[(t, hh, "vals")] = s_dv.n
                    continue

                @block.tensor
                def _(te, t=t, hh=hh, heads=heads, pieces=pieces):
                    te = TRK(te)
                    te.wait_ge(s_dv.h, A[(t, hh, "scaled")])
                    slot = 0
                    for h in heads:
                        for c in pieces:
                            ci = c - (t - 1)
                            bank = 3 * hh + slot // 8
                            te.transpose(pbb[bank][:, (slot % 8) * 128:
                                                   (slot % 8 + 1) * 128],
                                         p_buf[h % 8][:, ci * 128:(ci + 1) * 128],
                                         eye_b[:, :])
                            slot += 1
                    te.drain()
                    s_pe.inc(te.last)
                A[(t, hh, "pt")] = s_pe.n
                if attn_sub < 5:
                    A[(t, hh, "ptcopy")] = s_dv.n
                    A[(t, hh, "pv")] = s_pe.n
                    A[(t, hh, "vals")] = s_dv.n
                    continue

                @block.vector
                def _(v, t=t, hh=hh, heads=heads, pieces=pieces):
                    v = TRK(v)
                    v.wait_ge(s_pe.h, A[(t, hh, "pt")])
                    ppv = A.get((t, 0, "pv")) if hh == 1 else A.get((t - 1, 1, "pv"))
                    if ppv:
                        v.wait_ge(s_pe.h, ppv)
                    nslots = len(pieces) * 8
                    for b in range((nslots + 7) // 8):
                        w = min(8, nslots - b * 8)
                        v.tensor_copy(pt_sb[:, b * 1024:b * 1024 + w * 128],
                                      pbb[3 * hh + b][:, 0:w * 128])
                    v.drain()
                    s_dv.inc(v.last)
                A[(t, hh, "ptcopy")] = s_dv.n
                if attn_sub < 6:
                    A[(t, hh, "pv")] = s_pe.n
                    A[(t, hh, "vals")] = s_dv.n
                    continue

                @block.tensor
                def _(te, t=t, cw=cw, hh=hh, heads=heads, pieces=pieces):
                    te = TRK(te)
                    te.wait_ge(s_dv.h, A[(t, hh, "ptcopy")])
                    prev = A.get((t, 0, "vals")) if hh == 1 else \
                        A.get((t - 1, "wo_evac"))
                    if prev:
                        te.wait_ge(s_dv.h, prev)
                    slot = 0
                    for h in heads:
                        hl = h % 8
                        bank = 6 + hh
                        off = (hl // 2) * 128
                        pbase = 64 * (hl % 2)
                        for idx, c in enumerate(pieces):
                            kcnt = min(128, TLOC - c * 128)
                            te.matmul(pb[bank][pbase:pbase + 64, off:off + 128],
                                      v_tm[c][0:kcnt, h * 64:(h + 1) * 64],
                                      pt_sb[0:kcnt, slot * 128:(slot + 1) * 128],
                                      start=(idx == 0), stop=(idx == len(pieces) - 1))
                            slot += 1
                    te.drain()
                    s_pe.inc(te.last)
                A[(t, hh, "pv")] = s_pe.n

                @block.vector
                def _(v, t=t, hh=hh, heads=heads, pg=pg):
                    v = TRK(v)
                    v.wait_ge(s_pe.h, A[(t, hh, "pv")])
                    wo_prev = A.get((t - 1, "wo"))
                    if wo_prev:
                        v.wait_ge(s_pe.h, wo_prev)
                    for jl in range(4):
                        jt = 4 * hh + jl
                        v.tensor_copy(vals[:, jt * 128:jt * 128 + 128],
                                      pb[6 + hh][:, jl * 128:jl * 128 + 128])
                    v.drain()
                    s_dv.inc(v.last)
                A[(t, hh, "vals")] = s_dv.n

            if attn_sub < 7:
                A[(t, "wo")] = s_pe.n
                A[(t, "wo_evac")] = s_dv.n
                continue

            if t + 1 < attn_tmax:
                emit_qk(t + 1)
                emit_front(t + 1, (0,))

            @block.tensor
            def _(te, t=t, cw=cw, pg=pg):
                te = TRK(te)
                te.wait_ge(s_dv.h, A[(t, 1, "vals")])
                te.wait_ge(s_wo.h, wo_ready)
                for hf in range(2):
                    for j in range(KC):
                        te.matmul(pb[6 + hf][0:cw, 0:512],
                                  vals[:, j * 128:j * 128 + cw],
                                  wo_bf[:, j * 1024 + hf * 512:j * 1024 + (hf + 1) * 512],
                                  start=(j == 0), stop=(j == KC - 1))
                te.drain()
                s_pe.inc(te.last)
            A[(t, "wo")] = s_pe.n

            def emit_wo_evac(t=t, cw=cw):
                @block.vector
                def _(v, t=t, cw=cw):
                    v = TRK(v)
                    v.wait_ge(s_pe.h, A[(t, "wo")])
                    v.wait_ge(s_rb.h, bo_ready)
                    for hf in range(2):
                        v.scalar_tensor_tensor(y[t][0:cw, hf * 512:(hf + 1) * 512],
                                               pb[6 + hf][0:cw, 0:512], 1.0,
                                               y[t][0:cw, hf * 512:(hf + 1) * 512],
                                               ALU.mult, ALU.add)
                    v.drain()
                    v.tensor_tensor(y[t][0:cw, :], y[t][0:cw, :], bias_bc[0:cw, :],
                                    ALU.add)
                    v.drain()
                    s_dv.inc(v.last)
                A[(t, "wo_evac")] = s_dv.n
            pend_wo_evac[0] = emit_wo_evac
            if t == attn_tmax - 1:
                emit_wo_evac()
                pend_wo_evac[0] = None

        if attn_tmax > 0:
            z_done = {t: A[(min(t, attn_tmax - 1), "wo_evac")] for t in range(NCH)}
            attn_qk_last = A[(attn_tmax - 1, 1, "qk")]
            attn_pv_last = A[(attn_tmax - 1, 1, "pv")]
            attn_wo_last = A[(attn_tmax - 1, "wo")]
        else:
            z_done = {t: pz_done for t in range(NCH)}
            attn_qk_last = attn_pv_last = attn_wo_last = s_pe.n
    else:
        z_done = y1_done
        attn_qk_last = attn_pv_last = attn_wo_last = 0

    if "z" in tapd and stage in ("attn", "full"):
        @block.sync
        def _(sync):
            sync = TRK(sync)
            for t in range(NCH):
                sync.wait_ge(s_dv.h, z_done[t])
                s_st.inc(sync.dma_start(out=tapd["z"][t * 128:(t + 1) * 128, :],
                                        in_=y[t][:, :]), 16)

    if stage == "full":
        ln3_done = emit_layernorm(y, lnq,
                                  src_gate=lambda t: ((s_dv, z_done[t]),),
                                  dest_gate=attn_qk_last,
                                  bank_gate=z_done[NCH - 1],
                                  sq_pe_gate=attn_pv_last)
        lg2_last, chain2 = emit_router(
            ln3_done, z_done[NCH - 1], entry_pe=attn_wo_last)
        yfinal, _g, _d, _h = emit_ffn(1, ln3_done[NCH - 1],
                                      entry_pe=attn_wo_last,
                                      entry_dv0=z_done[NCH - 1],
                                      entry_dv1=max(z_done[NCH - 1], lg2_last),
                                      chain_fn=chain2)
    elif stage == "attn":
        yfinal = z_done
    else:
        yfinal = y1_done

    @block.sync
    def _(sync):
        sync = TRK(sync)
        for t in range(NCH):
            sync.wait_ge(s_dv.h, yfinal[t])
            if t == 0:
                s_st.inc(sync.dma_start(out=y_out[0:112, :], in_=y[0][16:128, :]), 16)
            elif t < 16:
                s_st.inc(sync.dma_start(out=y_out[112 + (t - 1) * 128:112 + t * 128, :],
                                        in_=y[t][:, :]), 16)
            else:
                s_st.inc(sync.dma_start(out=y_out[2032:2048, :], in_=y[16][0:16, :]), 16)
        sync.wait_ge(s_st.h, s_st.n)

    finals = {sem.h: sem.n for sem in
              (s_x, s_c, s_cg, s_pe, s_dv, s_ac, s_st, s_wd, s_wo, s_rb,
               *s_wg, *s_wq)}
    for h, val, stack in _WAITS:
        if h in finals and isinstance(val, int) and val > finals[h]:
            raise AssertionError(
                f"unreachable wait: sem {h.name} waited {val} > final {finals[h]}"
                f"  at {stack}")
    _WAITS.clear()

    es.close()
    return nc, tapd


def _halves(w):
    return np.ascontiguousarray(
        np.concatenate([w[:, 0:4096], w[:, 4096:8192]], axis=0))


def _prep_shared(inp):
    f32 = np.float32
    Wqkv = np.asarray(inp["L0_attn_Wqkv"], f32)
    bqkv = np.asarray(inp["L0_attn_bqkv"], f32)
    Wo = np.asarray(inp["L0_attn_Wo"], f32)
    bo = np.asarray(inp["L0_attn_bo"], f32)
    rw = np.asarray(inp["L0_ffn_router_w"], f32)
    wg_r = np.asarray(inp["L0_ffn_wg_r"], f32)
    wu_r = np.asarray(inp["L0_ffn_wu_r"], f32)
    wd_r = np.asarray(inp["L0_ffn_wd_r"], f32)
    wg_s = np.asarray(inp["L0_ffn_wg_s"], f32)
    wu_s = np.asarray(inp["L0_ffn_wu_s"], f32)
    wd_s = np.asarray(inp["L0_ffn_wd_s"], f32)

    f = np.arange(3 * D)
    r = f % (3 * HD)
    q_cols = f[r < HD]
    k_cols = f[(r >= HD) & (r < 2 * HD)]
    v_cols = f[r >= 2 * HD]
    Wq = Wqkv[:, q_cols]
    Wk = Wqkv[:, k_cols]
    Wv = Wqkv[:, v_cols]
    bq = bqkv[q_cols]
    bk = bqkv[k_cols]
    bv = bqkv[v_cols]

    wg_pe, wu_pe, wd_pe = [], [], []
    for n in range(2):
        for hf in range(2):
            wg_pe.append(wg_s[n][:, hf * 256:(hf + 1) * 256])
            wu_pe.append(wu_s[n][:, hf * 256:(hf + 1) * 256])
            wd_pe.append(wd_s[n][hf * 256:(hf + 1) * 256, :])
    for e in range(16):
        wg_pe.append(wg_r[e]); wu_pe.append(wu_r[e]); wd_pe.append(wd_r[e])

    wg_cat = np.concatenate([cm_layout(m) for m in wg_pe], axis=0)
    wu_cat = np.concatenate([cm_layout(m) for m in wu_pe], axis=0)
    wd_cat = np.concatenate(
        [cm_layout(np.concatenate([wd_pe[G * 4 + i] for i in range(4)], axis=0))
         for G in range(NG)], axis=0)

    return {
        "wg_cat": wg_cat, "wu_cat": wu_cat, "wd_cat": wd_cat,
        "rw_cm": cm_layout(rw),
        "wq_cm": _halves(cm_layout(Wq)), "wk_cm": _halves(cm_layout(Wk)),
        "wv_cm": _halves(cm_layout(Wv)), "wo_cm": cm_layout(Wo),
        "bq_pp": np.ascontiguousarray(bq.reshape(8, 128).T),
        "bk_pp": np.ascontiguousarray(bk.reshape(8, 128).T),
        "bvo_d": np.ascontiguousarray(np.stack([bv, bo])),
        "eyes_d": np.eye(128, dtype=f32),
    }


def _band_masks(parity):
    i = np.arange(128)[:, None]
    j = np.arange(160)[None, :]
    band = (j >= i) & (j <= i + 32)
    m_first = band.copy()
    m_mid = band.copy()
    m_last = band.copy()
    if parity == 0:
        m_first &= (j >= 32)
    else:
        m_last &= (j <= 31)
    packed = np.empty((128, 480), BFDT)
    for a, m in enumerate((m_first, m_mid, m_last)):
        mm = np.where(m, np.float32(0.0), np.float32(NEG))
        bad = ~m.any(axis=1)
        mm[bad, 0] = 0.0
        packed[:, a * 160:(a + 1) * 160] = mm.astype(BFDT)
    return packed


def kernel(**inputs):
    x = np.asarray(inputs["L0_attn_x"], np.float32)
    B, S, _ = x.shape
    shared = _prep_shared(inputs)

    in_maps = []
    for c in range(8):
        b, half = c // 2, c % 2
        qs = half * 2048
        xl = np.zeros((TLOC, D), np.float32)
        lo, hi = qs - 16, qs + 2064
        slo, shi = max(lo, 0), min(hi, S)
        xl[slo - lo:shi - lo, :] = x[b, slo:shi, :]
        m = dict(shared)
        m["x_l"] = xl
        m["masks_d"] = _band_masks(half)
        in_maps.append(m)

    nc, _ = build_program()
    res = run_bass_kernel_spmd(nc, in_maps, core_ids=list(range(8)))
    out = np.empty((B, S, D), np.float32)
    for c in range(8):
        b, half = c // 2, c % 2
        out[b, half * 2048:(half + 1) * 2048, :] = res.results[c]["y_out"]
    return out


# version 17
# speedup vs baseline: 1.0120x; 1.0120x over previous
import numpy as np
import ml_dtypes
import concourse.bass as bass
import concourse.mybir as mybir
from concourse.bass_utils import run_bass_kernel_spmd
from contextlib import ExitStack

F32 = mybir.dt.float32
BF16 = mybir.dt.bfloat16
U8 = mybir.dt.uint8
AF = mybir.ActivationFunctionType
ALU = mybir.AluOpType
AX = mybir.AxisListType
BFDT = ml_dtypes.bfloat16

D = 1024
H = 16
HD = 64
TLOC = 2080
NCH = 17
KC = 8
NST = 5
STW = 416
NPE = 20
NG = 5
NEG = -1.0e30

STAGE = "full"
DEBUG_TAPS = ()


class Sem:
    def __init__(self, nc, es, name):
        self.h = es.enter_context(nc.semaphore(name))
        self.n = 0

    def inc(self, instr, k=1):
        instr.then_inc(self.h, k)
        self.n += k
        _EVENTS.append(("last", "inc", self.h.name, k))
        return self.n


_WAITS = []
_EVENTS = []


class TRK:

    def __init__(self, eng):
        self._e = eng
        self.last = None

    def __getattr__(self, name):
        a = getattr(self._e, name)
        if callable(a):
            def call(*args, **kw):
                eng = getattr(self._e, "engine", "?")
                if name == "wait_ge" and len(args) >= 2:
                    import traceback
                    _WAITS.append((args[0], args[1],
                                   traceback.extract_stack()[-3:-1]))
                    _EVENTS.append((str(eng), "wait", args[0].name, args[1]))
                r = a(*args, **kw)
                if hasattr(r, "then_inc"):
                    self.last = r
                    _EVENTS.append((str(eng), "op", None, None))
                return r
            return call
        return a


def cm_layout(wmat):
    K, M = wmat.shape
    assert K % 128 == 0
    return np.ascontiguousarray(
        wmat.reshape(K // 128, 128, M).transpose(1, 0, 2).reshape(128, -1))


def build_program(stage=None, taps=None):
    stage = stage or STAGE
    taps = taps if taps is not None else DEBUG_TAPS
    nc = bass.Bass("TRN2", target_bir_lowering=False,
                   dynamic_dma_scratch_size=16384)

    x_l = nc.dram_tensor("x_l", [TLOC, D], F32, kind="ExternalInput").ap()
    wg_cat = nc.dram_tensor("wg_cat", [NPE * 128, 2048], F32, kind="ExternalInput").ap()
    wu_cat = nc.dram_tensor("wu_cat", [NPE * 128, 2048], F32, kind="ExternalInput").ap()
    wd_cat = nc.dram_tensor("wd_cat", [NG * 128, 8192], F32, kind="ExternalInput").ap()
    rw_cm = nc.dram_tensor("rw_cm", [128, 128], F32, kind="ExternalInput").ap()
    wq_cm = nc.dram_tensor("wq_cm", [2 * 128, 4096], F32, kind="ExternalInput").ap()
    wk_cm = nc.dram_tensor("wk_cm", [2 * 128, 4096], F32, kind="ExternalInput").ap()
    wv_cm = nc.dram_tensor("wv_cm", [2 * 128, 4096], F32, kind="ExternalInput").ap()
    wo_cm = nc.dram_tensor("wo_cm", [128, 8192], F32, kind="ExternalInput").ap()
    bq_pp = nc.dram_tensor("bq_pp", [128, 8], F32, kind="ExternalInput").ap()
    bk_pp = nc.dram_tensor("bk_pp", [128, 8], F32, kind="ExternalInput").ap()
    bvo_d = nc.dram_tensor("bvo_d", [2, D], F32, kind="ExternalInput").ap()
    masks_d = nc.dram_tensor("masks_d", [128, 480], BF16, kind="ExternalInput").ap()
    eyes_d = nc.dram_tensor("eyes_d", [128, 128], F32, kind="ExternalInput").ap()
    y_out = nc.dram_tensor("y_out", [2048, D], F32, kind="ExternalOutput").ap()

    tapd = {}
    for tname, shape, dt in (("ln1", [KC * 128, TLOC], BF16),
                             ("comb", [16, TLOC], BF16),
                             ("y1", [NCH * 128, D], F32),
                             ("hg", [KC * 128, TLOC], BF16),
                             ("wd", [128, 8192], BF16),
                             ("cb", [2 * 128, 2080], BF16),
                             ("z", [NCH * 128, D], F32)):
        if tname in taps:
            tapd[tname] = nc.dram_tensor("tap_" + tname, shape, dt,
                                         kind="ExternalOutput").ap()

    es = ExitStack()
    def sbt(name, shape, dtype):
        return es.enter_context(nc.sbuf_tensor(name, shape, dtype))
    def pst(name, shape, dtype=F32):
        return es.enter_context(nc.psum_tensor(name, shape, dtype))

    y = [sbt(f"y{t}", [128, D], F32) for t in range(NCH)]
    lnq = [sbt(f"lnq{j}", [128, 2080], BF16) for j in range(KC)]
    hv = sbt("hv", [128, 34816], U8)
    h_g = [hv.ap()[:, j * 4352:(j + 1) * 4352].bitcast(BF16) for j in range(KC)]
    v_tm = [hv.ap()[:, t * 2048:(t + 1) * 2048].bitcast(BF16) for t in range(NCH)]
    dstrm = sbt("dstrm", [128, 33536], U8)
    wd_bf = dstrm.ap()[:, 0:16384].bitcast(BF16)
    wgu_slot = [dstrm.ap()[:, 16384 + i * 4096:16384 + (i + 1) * 4096].bitcast(BF16)
                for i in range(4)]
    k_fm = [dstrm.ap()[:, j * 4192:(j + 1) * 4192].bitcast(BF16) for j in range(KC)]
    earena = sbt("earena", [128, 24704], U8)
    lg_all = earena.ap()[:, 0:1088].bitcast(F32)
    comb_tm = earena.ap()[:, 1088:2176].bitcast(F32)
    rstat2 = earena.ap()[:, 2176:2688].bitcast(F32)
    msk_all = earena.ap()[:, 2688:3776].bitcast(F32)
    exps_all = earena.ap()[:, 3776:4864].bitcast(F32)
    wqkv_slot = [earena.ap()[:, i * 8192:(i + 1) * 8192].bitcast(BF16) for i in range(3)]
    p_buf = [earena.ap()[:, h * 768:(h + 1) * 768].bitcast(BF16) for h in range(8)]
    wo_bf = earena.ap()[:, 6144:22528].bitcast(BF16)

    lnstage = sbt("lnstage", [128, D], BF16)
    pt_sb = sbt("pt_sb", [128, 3072], BF16)
    sqscr = pt_sb.ap().bitcast(F32)[:, 0:D]
    lnstage2 = pt_sb.ap()[:, 2048:3072]
    stats = sbt("stats", [128, 64], F32)
    eps_t = sbt("eps_t", [128, 1], F32)
    rw_bf = sbt("rw_bf", [128, 128], BF16)
    sg_st = earena.ap()[:, 21376:21376 + 2 * 4 * STW].bitcast(BF16)
    mask_t = sbt("mask_t", [128, 480], BF16)
    eye_b = sbt("eye_b", [128, 128], BF16)
    bias_bc = sbt("bias_bc", [128, D], F32)
    bq_t = sbt("bq_t", [128, 8], F32)
    bk_t = sbt("bk_t", [128, 8], F32)
    stage_s = earena.ap()[:, 22528:23808].bitcast(F32)
    astat = earena.ap()[:, 23808:24192].bitcast(F32)
    vals = sbt("vals", [128, 1024], BF16)

    pb = [pst(f"pb{i}", [128, 512], F32) for i in range(8)]
    pbb = [p.ap().bitcast(BF16) for p in pb]

    s_x = Sem(nc, es, "s_x")
    s_c = Sem(nc, es, "s_c")
    s_cg = Sem(nc, es, "s_cg")
    s_pe = Sem(nc, es, "s_pe")
    s_dv = Sem(nc, es, "s_dv")
    s_ac = Sem(nc, es, "s_ac")
    s_st = Sem(nc, es, "s_st")
    s_wd = Sem(nc, es, "s_wd")
    s_wg = [Sem(nc, es, f"s_wg{i}") for i in range(2)]
    s_wq = [Sem(nc, es, f"s_wq{i}") for i in range(3)]
    s_wo = Sem(nc, es, "s_wo")
    s_rb = Sem(nc, es, "s_rb")

    block = es.enter_context(nc.Block())

    x_done = [0] * NCH

    @block.sync
    def _(sync):
        sync = TRK(sync)
        for t in range(NCH):
            r0, r1 = t * 128, min((t + 1) * 128, TLOC)
            s_x.inc(sync.dma_start(out=y[t][0:r1 - r0, :], in_=x_l[r0:r1, :]), 16)
        for t in range(NCH):
            x_done[t] = s_x.n
        s_c.inc(sync.dma_start(out=mask_t[:, :], in_=masks_d), 16)
        s_c.inc(sync.dma_start(out=bq_t[:, :], in_=bq_pp), 16)
        s_c.inc(sync.dma_start(out=bk_t[:, :], in_=bk_pp), 16)

    @block.gpsimd
    def _(g):
        g = TRK(g)
        s_cg.inc(g.dma_start(out=eye_b[:, :], in_=eyes_d[:, 0:128]), 16)
        s_cg.inc(g.dma_start(out=rw_bf[:, :], in_=rw_cm), 16)
    consts_done = s_c.n
    cg_done = s_cg.n

    @block.vector
    def _(v):
        v = TRK(v)
        v.memset(eps_t[:, :], 1e-5)
        v.memset(rstat2[:, :], 0.0)
        v.memset(lg_all[:, :], 0.0)
        v.wait_ge(s_x.h, x_done[NCH - 1])
        v.memset(y[NCH - 1][32:64, :], 0.0)
        v.memset(y[NCH - 1][64:128, :], 0.0)
        v.drain()
        s_dv.inc(v.last)
    junk_done = s_dv.n

    def emit_layernorm(src, dest_tiles, src_gate, dest_gate, bank_gate,
                       sq_pe_gate=0):
        @block.vector
        def _(v):
            v = TRK(v)
            for t in range(NCH):
                for gs, gv in (src_gate(t) or ()):
                    v.wait_ge(gs.h, gv)
                v.reduce_sum(stats[:, t:t + 1], src[t][:, :], axis=AX.X)
            v.drain()
            s_dv.inc(v.last)
        sums_done = s_dv.n

        @block.scalar
        def _(sc):
            sc = TRK(sc)
            if sq_pe_gate:
                sc.wait_ge(s_pe.h, sq_pe_gate)
            for gs, gv in (src_gate(7) or ()):
                sc.wait_ge(gs.h, gv)
            for t in range(8):
                sc.activation(sqscr[:, :], src[t][:, :], AF.Square,
                              accum_out=stats[:, 17 + t:18 + t])
                sc.drain()
            for gs, gv in (src_gate(NCH - 1) or ()):
                sc.wait_ge(gs.h, gv)
            for t in range(8, NCH):
                sc.activation(sqscr[:, :], src[t][:, :], AF.Square,
                              accum_out=stats[:, 17 + t:18 + t])
                sc.drain()
            s_ac.inc(sc.last)
        sqs_done = s_ac.n

        @block.vector
        def _(v):
            v = TRK(v)
            v.wait_ge(s_dv.h, sums_done)
            v.wait_ge(s_ac.h, sqs_done)
            v.wait_ge(s_c.h, consts_done)
            v.wait_ge(s_cg.h, cg_done)
            m = stats[:, 34:51]; q2 = stats[:, 17:34]; s1 = stats[:, 0:17]
            v.tensor_scalar_mul(m, s1, 1.0 / D)
            v.tensor_scalar_mul(q2, q2, 1.0 / D)
            v.drain()
            v.tensor_tensor(s1, m, m, ALU.mult)
            v.drain()
            v.tensor_tensor(q2, q2, s1, ALU.subtract)
            v.drain()
            s_dv.inc(v.last)
        var_done = s_dv.n

        @block.scalar
        def _(sc):
            sc = TRK(sc)
            sc.wait_ge(s_dv.h, var_done)
            sc.activation(stats[:, 0:17], stats[:, 17:34], AF.Ln, bias=eps_t[:, 0:1])
            sc.drain()
            sc.activation(stats[:, 17:34], stats[:, 0:17], AF.Exp, scale=-0.5)
            sc.drain()
            s_ac.inc(sc.last)
        rstd_done = s_ac.n

        @block.vector
        def _(v):
            v = TRK(v)
            v.wait_ge(s_ac.h, rstd_done)
            m = stats[:, 34:51]
            v.tensor_tensor(m, m, stats[:, 17:34], ALU.mult)
            v.drain()
            v.tensor_scalar_mul(m, m, -1.0)
            v.drain()
            s_dv.inc(v.last)
        ab_done = s_dv.n

        lnT = [0] * NCH
        lnC = [0] * NCH
        app = [0] * NCH
        for t in range(NCH):
            pg = t % 2
            lnst = lnstage if t % 2 == 0 else lnstage2

            @block.scalar
            def _(sc, t=t, pg=pg, lnst=lnst):
                sc = TRK(sc)
                sc.wait_ge(s_dv.h, ab_done)
                if t >= 2:
                    sc.wait_ge(s_pe.h, lnT[t - 2])
                sc.activation(lnst[:, :], src[t][:, :], AF.Identity,
                              bias=stats[:, 34 + t:35 + t],
                              scale=stats[:, 17 + t:18 + t])
                sc.drain()
                s_ac.inc(sc.last)
            app[t] = s_ac.n

            @block.tensor
            def _(te, t=t, pg=pg, lnst=lnst):
                te = TRK(te)
                te.wait_ge(s_ac.h, app[t])
                if t < 2:
                    if bank_gate:
                        te.wait_ge(s_dv.h, bank_gate)
                else:
                    te.wait_ge(s_dv.h, lnC[t - 2])
                for j in range(KC):
                    te.transpose(pbb[pg][:, j * 128:(j + 1) * 128],
                                 lnst[:, j * 128:(j + 1) * 128],
                                 eye_b[:, :])
                te.drain()
                s_pe.inc(te.last)
            lnT[t] = s_pe.n

            @block.vector
            def _(v, t=t, pg=pg):
                v = TRK(v)
                v.wait_ge(s_pe.h, lnT[t])
                if t == 0 and dest_gate:
                    v.wait_ge(s_pe.h, dest_gate)
                w = min(128, TLOC - t * 128)
                for j in range(KC):
                    v.tensor_copy(dest_tiles[j][:, t * 128:t * 128 + w],
                                  pbb[pg][:, j * 128:j * 128 + w])
                v.drain()
                s_dv.inc(v.last)
            lnC[t] = s_dv.n
        return lnC

    ln1_done = emit_layernorm(y, lnq,
                              src_gate=lambda t: ((s_x, x_done[t]), (s_dv, junk_done)),
                              dest_gate=0, bank_gate=0)
    ln1_all = ln1_done[NCH - 1]

    if "ln1" in tapd:
        @block.sync
        def _(sync):
            sync = TRK(sync)
            sync.wait_ge(s_dv.h, ln1_all)
            for j in range(KC):
                s_st.inc(sync.dma_start(out=tapd["ln1"][j * 128:(j + 1) * 128, :],
                                        in_=lnq[j][:, 0:TLOC]), 16)

    def emit_router(ln_done, entry_dv, entry_pe=0):
        @block.tensor
        def _(te):
            te = TRK(te)
            te.wait_ge(s_cg.h, cg_done)
            te.wait_ge(s_dv.h, ln_done[NCH - 1])
            if entry_dv:
                te.wait_ge(s_dv.h, entry_dv)
            for t in range(NCH):
                cw = min(128, TLOC - t * 128)
                for k in range(KC):
                    te.matmul(pb[6][0:cw, t * 16:t * 16 + 16],
                              lnq[k][:, t * 128:t * 128 + cw],
                              rw_bf[:, k * 16:(k + 1) * 16],
                              start=(k == 0), stop=(k == KC - 1))
            te.drain()
            s_pe.inc(te.last)
        lg_done = s_pe.n

        @block.vector
        def _(v):
            v = TRK(v)
            v.wait_ge(s_pe.h, lg_done)
            if entry_pe:
                v.wait_ge(s_pe.h, entry_pe)
            v.tensor_copy(lg_all[:, 0:256], pb[6][:, 0:256])
            v.tensor_copy(lg_all[0:32, 256:272], pb[6][0:32, 256:272])
            v.drain()
            s_dv.inc(v.last)
        lgE = [s_dv.n] * NCH

        Zc = rstat2[:, 0:17]
        e2c = rstat2[:, 17:34]
        denc = rstat2[:, 34:51]
        w1c = rstat2[:, 51:68]
        w2c = rstat2[:, 68:85]
        nm1c = rstat2[:, 85:102]
        nm2c = rstat2[:, 102:119]
        cwt = [min(128, TLOC - t * 128) for t in range(NCH)]

        def Lt(t):
            return lg_all[0:cwt[t], t * 16:(t + 1) * 16]

        def Ct(t):
            return comb_tm[0:cwt[t], t * 16:(t + 1) * 16]

        def Mt(t):
            return msk_all[0:cwt[t], t * 16:(t + 1) * 16]

        def chain():
            @block.vector
            def _(v):
                v = TRK(v)
                v.wait_ge(s_dv.h, lgE[NCH - 1])
                for t in range(NCH):
                    v.reduce_max(nm1c[0:cwt[t], t:t + 1], Lt(t), axis=AX.X,
                                 negate=True)
                v.drain()
                for t in range(NCH):
                    v.tensor_scalar(Ct(t), Lt(t), nm1c[0:cwt[t], t:t + 1], 0.0,
                                    ALU.add, ALU.is_ge)
                v.drain()
                for t in range(NCH):
                    v.scalar_tensor_tensor(Mt(t), Ct(t), NEG, Lt(t),
                                           ALU.mult, ALU.add)
                v.drain()
                for t in range(NCH):
                    v.reduce_max(nm2c[0:cwt[t], t:t + 1], Mt(t), axis=AX.X,
                                 negate=True)
                v.drain()
                s_dv.inc(v.last)
            mx_done = s_dv.n

            @block.scalar
            def _(sc):
                sc = TRK(sc)
                sc.wait_ge(s_dv.h, mx_done)
                for t in range(NCH):
                    sc.activation(exps_all[0:cwt[t], t * 16:(t + 1) * 16], Lt(t),
                                  AF.Exp, bias=nm1c[0:cwt[t], t:t + 1])
                for t in range(NCH):
                    sc.activation(e2c[0:cwt[t], t:t + 1],
                                  nm2c[0:cwt[t], t:t + 1], AF.Exp,
                                  bias=nm1c[0:cwt[t], t:t + 1], scale=-1.0)
                sc.drain()
                s_ac.inc(sc.last)
            ez_done = s_ac.n

            @block.vector
            def _(v):
                v = TRK(v)
                v.wait_ge(s_ac.h, ez_done)
                for t in range(NCH):
                    v.reduce_sum(Zc[0:cwt[t], t:t + 1],
                                 exps_all[0:cwt[t], t * 16:(t + 1) * 16],
                                 axis=AX.X)
                v.drain()
                v.tensor_scalar(denc, Zc, 1e-9, 1.0, ALU.mult, ALU.add)
                v.drain()
                v.tensor_tensor(denc, denc, e2c, ALU.add)
                v.drain()
                v.reciprocal(w1c, denc)
                v.drain()
                v.tensor_scalar_mul(w1c, w1c, 0.5)
                v.drain()
                v.tensor_tensor(w2c, e2c, w1c, ALU.mult)
                v.drain()
                for t in range(NCH):
                    v.tensor_scalar_mul(Ct(t), Ct(t), w1c[0:cwt[t], t:t + 1])
                for t in range(NCH):
                    v.tensor_scalar(Mt(t), Mt(t), nm2c[0:cwt[t], t:t + 1], 0.0,
                                    ALU.add, ALU.is_ge)
                v.drain()
                for t in range(NCH):
                    v.tensor_scalar_mul(Mt(t), Mt(t), w2c[0:cwt[t], t:t + 1])
                v.drain()
                for t in range(NCH):
                    v.tensor_tensor(Ct(t), Ct(t), Mt(t), ALU.add)
                v.drain()
                s_dv.inc(v.last)
            return s_dv.n

        return lgE[NCH - 1], chain

    if stage != "ln":
        lg1_last, chain1 = emit_router(ln1_done, 0)
    else:
        lg1_last, chain1 = 0, (lambda: 0)

    def emit_ffn(call, ln_all, entry_pe, entry_dv0, entry_dv1, chain_fn):
        entry_dv = entry_dv1
        gu_done = {}
        gu_st = {}
        hst = {}
        h_done = {}
        wload = {}
        evac = {}
        last_slot_user = {0: 0, 1: 0}
        prev_bank_user = {0: entry_dv0, 1: entry_dv1}
        dn_last = [entry_pe]
        comb_ready = [0]

        def emit_wgu(e):
            sl = e % 2

            @block.gpsimd
            def _(g, e=e, sl=sl):
                g = TRK(g)
                if e >= 2:
                    g.wait_ge(s_pe.h, gu_done[e - 2])
                elif entry_pe:
                    g.wait_ge(s_pe.h, entry_pe)
                s_wg[sl].inc(g.dma_start(out=wgu_slot[sl][:, :],
                                         in_=wg_cat[e * 128:(e + 1) * 128, :]), 16)
                s_wg[sl].inc(g.dma_start(out=wgu_slot[2 + sl][:, :],
                                         in_=wu_cat[e * 128:(e + 1) * 128, :]), 16)
            wload[e] = s_wg[sl].n

        for G in range(NG):
            emit_wgu(G * 4)
            emit_wgu(G * 4 + 1)

            @block.gpsimd
            def _(g, G=G):
                g = TRK(g)
                g.wait_ge(s_pe.h, dn_last[0])
                s_wd.inc(g.dma_start(out=wd_bf[:, :],
                                     in_=wd_cat[G * 128:(G + 1) * 128, :]), 16)
            wd_ready = s_wd.n

            for i in range(4):
                e = G * 4 + i
                sl = e % 2
                if i >= 2:
                    emit_wgu(e)

                for st in range(NST):
                    par = (e * NST + st) % 2
                    bs = par * 4

                    @block.tensor
                    def _(te, e=e, st=st, bs=bs, sl=sl, par=par):
                        te = TRK(te)
                        if st == 0:
                            te.wait_ge(s_wg[sl].h, wload[e])
                            te.wait_ge(s_dv.h, ln_all)
                        g = prev_bank_user[par]
                        if g:
                            te.wait_ge(s_dv.h, g)
                        elif entry_dv:
                            te.wait_ge(s_dv.h, entry_dv)
                        for mc in range(2):
                            for k in range(KC):
                                te.matmul(pb[bs + mc][:, 0:STW],
                                          wgu_slot[sl][:, k * 256 + mc * 128:
                                                       k * 256 + (mc + 1) * 128],
                                          lnq[k][:, st * STW:(st + 1) * STW],
                                          start=(k == 0), stop=(k == KC - 1))
                        for mc in range(2):
                            for k in range(KC):
                                te.matmul(pb[bs + 2 + mc][:, 0:STW],
                                          wgu_slot[2 + sl][:, k * 256 + mc * 128:
                                                           k * 256 + (mc + 1) * 128],
                                          lnq[k][:, st * STW:(st + 1) * STW],
                                          start=(k == 0), stop=(k == KC - 1))
                        te.drain()
                        s_pe.inc(te.last)
                    gu_st[(e, st)] = s_pe.n

                    @block.scalar
                    def _(sc, e=e, st=st, bs=bs, par=par):
                        sc = TRK(sc)
                        sc.wait_ge(s_pe.h, gu_st[(e, st)])
                        if last_slot_user[par]:
                            sc.wait_ge(s_dv.h, last_slot_user[par])
                        for mc in range(2):
                            sc.activation(sg_st[:, (2 * par + mc) * STW:
                                                (2 * par + mc + 1) * STW],
                                          pb[bs + mc][:, 0:STW], AF.Silu)
                        sc.drain()
                        s_ac.inc(sc.last)
                    silu_done = s_ac.n

                    @block.vector
                    def _(v, e=e, st=st, bs=bs, par=par, i=i,
                          silu_done=silu_done):
                        v = TRK(v)
                        v.wait_ge(s_ac.h, silu_done)
                        for mc in range(2):
                            dst = h_g[2 * i + mc][:, st * STW:(st + 1) * STW]
                            v.tensor_tensor(dst,
                                            sg_st[:, (2 * par + mc) * STW:
                                                  (2 * par + mc + 1) * STW],
                                            pb[bs + 2 + mc][:, 0:STW], ALU.mult)
                        v.drain()
                        s_dv.inc(v.last)
                    hst[(e, st)] = s_dv.n
                    last_slot_user[par] = s_dv.n
                    prev_bank_user[par] = s_dv.n
                gu_done[e] = gu_st[(e, NST - 1)]
                h_done[e] = s_dv.n
                if e % (4 * NG) == 1 and chain_fn is not None:
                    comb_ready[0] = chain_fn()

            if G == 0:
                dn_mm = {}
                for t in range(NCH):
                    cw = min(128, TLOC - t * 128)
                    bk = 4 if t % 2 == 0 else 6

                    @block.tensor
                    def _(te, G=G, t=t, cw=cw, bk=bk):
                        te = TRK(te)
                        te.wait_ge(s_dv.h, h_done[G * 4 + 3])
                        te.wait_ge(s_wd.h, wd_ready)
                        if t >= 2:
                            te.wait_ge(s_dv.h, evac[(G, t - 2)])
                        elif entry_dv:
                            te.wait_ge(s_dv.h, entry_dv)
                        for hf in range(2):
                            for k in range(KC):
                                te.matmul(pb[bk + hf][0:cw, 0:512],
                                          h_g[k][:, t * 128:t * 128 + cw],
                                          wd_bf[:, k * 1024 + hf * 512:
                                                k * 1024 + (hf + 1) * 512],
                                          start=(k == 0), stop=(k == KC - 1))
                        te.drain()
                        s_pe.inc(te.last)
                    dn_mm[t] = s_pe.n

                    @block.vector
                    def _(v, G=G, t=t, cw=cw, bk=bk):
                        v = TRK(v)
                        v.wait_ge(s_pe.h, dn_mm[t])
                        for hf in range(2):
                            v.scalar_tensor_tensor(y[t][0:cw, hf * 512:(hf + 1) * 512],
                                                   pb[bk + hf][0:cw, 0:512], 0.5,
                                                   y[t][0:cw, hf * 512:(hf + 1) * 512],
                                                   ALU.mult, ALU.add)
                        v.drain()
                        s_dv.inc(v.last)
                    evac[(G, t)] = s_dv.n
                dn_last[0] = dn_mm[NCH - 1]
                dn_mm_last = dn_mm[NCH - 1]
                prev_bank_user[1] = max(prev_bank_user[1], evac[(G, NCH - 1)])
            else:
                seq = 0
                dn_u = {}
                for t in range(NCH):
                    cw = min(128, TLOC - t * 128)
                    for ei in range(4):
                        pair = 4 + 2 * (seq % 2)

                        @block.tensor
                        def _(te, G=G, t=t, cw=cw, ei=ei, pair=pair, seq=seq):
                            te = TRK(te)
                            if seq == 0:
                                te.wait_ge(s_dv.h, h_done[G * 4 + 3])
                                te.wait_ge(s_wd.h, wd_ready)
                            if seq >= 2:
                                te.wait_ge(s_dv.h, evac[(G, seq - 2)])
                            elif G == 1:
                                te.wait_ge(s_dv.h, evac[(0, NCH - 1)])
                            else:
                                te.wait_ge(s_dv.h, evac[(G - 1, 4 * NCH - 1)])
                            for hf in range(2):
                                for k2 in range(2):
                                    k = 2 * ei + k2
                                    te.matmul(pb[pair + hf][0:cw, 0:512],
                                              h_g[k][:, t * 128:t * 128 + cw],
                                              wd_bf[:, k * 1024 + hf * 512:
                                                    k * 1024 + (hf + 1) * 512],
                                              start=(k2 == 0), stop=(k2 == 1))
                            te.drain()
                            s_pe.inc(te.last)
                        dn_u[seq] = s_pe.n

                        @block.vector
                        def _(v, G=G, t=t, cw=cw, ei=ei, pair=pair, seq=seq):
                            v = TRK(v)
                            v.wait_ge(s_pe.h, dn_u[seq])
                            r = G * 4 + ei - 4
                            ccol = comb_tm[0:cw, t * 16 + r:t * 16 + r + 1]
                            for hf in range(2):
                                v.scalar_tensor_tensor(
                                    y[t][0:cw, hf * 512:(hf + 1) * 512],
                                    pb[pair + hf][0:cw, 0:512], ccol,
                                    y[t][0:cw, hf * 512:(hf + 1) * 512],
                                    ALU.mult, ALU.add)
                            v.drain()
                            s_dv.inc(v.last)
                        evac[(G, seq)] = s_dv.n
                        seq += 1
                dn_last[0] = dn_u[4 * NCH - 1]
                dn_mm_last = dn_u[4 * NCH - 1]
                prev_bank_user[1] = max(prev_bank_user[1], evac[(G, 4 * NCH - 1)])

        return ({t: evac[(NG - 1, 4 * t + 3)] for t in range(NCH)},
                gu_done[NG * 4 - 1], dn_mm_last, h_done[NG * 4 - 1])

    if stage != "ln":
        y1_done, ffn1_gu_last, ffn1_dn_last, ffn1_h_last = \
            emit_ffn(0, ln1_all, 0, 0, lg1_last, chain1)
    else:
        y1_done = {t: junk_done for t in range(NCH)}
        ffn1_gu_last = ffn1_dn_last = ffn1_h_last = 0

    if "wd" in tapd and stage != "ln":
        @block.sync
        def _(sync):
            sync = TRK(sync)
            sync.wait_ge(s_pe.h, ffn1_dn_last)
            s_st.inc(sync.dma_start(out=tapd["wd"], in_=wd_bf[:, :]), 16)

    if "hg" in tapd and stage != "ln":
        @block.sync
        def _(sync):
            sync = TRK(sync)
            sync.wait_ge(s_dv.h, ffn1_h_last)
            for j in range(KC):
                s_st.inc(sync.dma_start(out=tapd["hg"][j * 128:(j + 1) * 128, :],
                                        in_=h_g[j][:, 0:TLOC]), 16)

    if "y1" in tapd and stage != "ln":
        @block.sync
        def _(sync):
            sync = TRK(sync)
            for t in range(NCH):
                sync.wait_ge(s_dv.h, y1_done[t])
                s_st.inc(sync.dma_start(out=tapd["y1"][t * 128:(t + 1) * 128, :],
                                        in_=y[t][:, :]), 16)

    if stage in ("attn", "full"):
        ln2_done = emit_layernorm(y, lnq,
                                  src_gate=lambda t: ((s_dv, y1_done[t]),),
                                  dest_gate=ffn1_gu_last,
                                  bank_gate=y1_done[NCH - 1])
        ln2_all = ln2_done[NCH - 1]

        @block.vector
        def _(v):
            v = TRK(v)
            v.wait_ge(s_pe.h, ffn1_dn_last)
            for j in range(KC):
                v.memset(k_fm[j][:, 0:16], 0.0)
            v.drain()
            s_dv.inc(v.last)
        kz_done = s_dv.n

        part = {}

        def stream_part(name, src, half, slot, gate_pe, gate_dv=0):
            @block.gpsimd
            def _(g):
                g = TRK(g)
                if gate_pe:
                    g.wait_ge(s_pe.h, gate_pe)
                if gate_dv:
                    g.wait_ge(s_dv.h, gate_dv)
                s_wq[slot].inc(g.dma_start(out=wqkv_slot[slot][:, :],
                                           in_=src[half * 128:(half + 1) * 128, :]), 16)
            part[(name, half)] = (slot, s_wq[slot].n)

        stream_part("k", wk_cm, 0, 0, 0, y1_done[NCH - 1])
        stream_part("k", wk_cm, 1, 1, 0, 0)
        stream_part("v", wv_cm, 0, 2, 0, ffn1_h_last)

        kq_evac = {}
        last_mm = {}

        def emit_proj(name, dest, bias, extra_gate_dv):
            for st in range(NST):
                lms = {}
                for mh in range(2):
                    @block.tensor
                    def _(te, name=name, st=st, mh=mh):
                        te = TRK(te)
                        if st == 0 and mh == 0:
                            for half in range(2):
                                slot_h, thr_h = part[(name, half)]
                                te.wait_ge(s_wq[slot_h].h, thr_h)
                            te.wait_ge(s_dv.h, ln2_all)
                            if extra_gate_dv:
                                te.wait_ge(s_dv.h, extra_gate_dv)
                        if st >= 1:
                            te.wait_ge(s_ac.h, kq_evac[(name, st - 1, mh)])
                        for m in range(mh * 4, mh * 4 + 4):
                            for k in range(KC):
                                slot = part[(name, k // 4)][0]
                                te.matmul(pb[m][:, 0:STW],
                                          wqkv_slot[slot][:, (k % 4) * 1024 + m * 128:
                                                          (k % 4) * 1024 + (m + 1) * 128],
                                          lnq[k][:, st * STW:(st + 1) * STW],
                                          start=(k == 0), stop=(k == KC - 1))
                        te.drain()
                        s_pe.inc(te.last)
                    last_mm[name] = s_pe.n
                    lms[mh] = s_pe.n

                def emit_evac(mh, mm_lm, name=name, st=st, dest=dest, bias=bias):
                    @block.scalar
                    def _(sc, name=name, st=st, mh=mh, dest=dest, bias=bias,
                          mm_lm=mm_lm):
                        sc = TRK(sc)
                        sc.wait_ge(s_pe.h, mm_lm)
                        if name == "k":
                            sc.wait_ge(s_dv.h, kz_done)
                        for m in range(mh * 4, mh * 4 + 4):
                            if name == "k":
                                out = dest[m][:, 16 + st * STW:16 + (st + 1) * STW]
                            else:
                                out = dest[m][:, st * STW:(st + 1) * STW]
                            sc.activation(out, pb[m][:, 0:STW], AF.Identity,
                                          bias=bias[:, m:m + 1])
                        sc.drain()
                        s_ac.inc(sc.last)
                    kq_evac[(name, st, mh)] = s_ac.n

                for mh in range(2):
                    emit_evac(mh, lms[1] if name == "q" else lms[mh])

        emit_proj("k", k_fm, bk_t, y1_done[NCH - 1])
        k_proj_done_pe = last_mm["k"]
        k_evac_all = kq_evac[("k", NST - 1, 1)]

        stream_part("v", wv_cm, 1, 0, k_proj_done_pe)

        @block.gpsimd
        def _(g):
            g = TRK(g)
            s_rb.inc(g.dma_start(out=bias_bc[:, :],
                                 in_=bvo_d[0:1, :].partition_broadcast(128)), 16)
        bv_ready = s_rb.n
        v_mm = {}
        v_evac = {}
        for t in range(NCH):
            cw = min(128, TLOC - t * 128)
            bank = 4 + 2 * (t % 2)

            @block.tensor
            def _(te, t=t, cw=cw, bank=bank):
                te = TRK(te)
                for half in range(2):
                    slot_h, thr_h = part[("v", half)]
                    te.wait_ge(s_wq[slot_h].h, thr_h)
                te.wait_ge(s_c.h, consts_done)
                te.wait_ge(s_cg.h, cg_done)
                te.wait_ge(s_dv.h, ln2_all)
                if t >= 2:
                    te.wait_ge(s_dv.h, v_evac[t - 2])
                else:
                    te.wait_ge(s_ac.h, k_evac_all)
                for hf in range(2):
                    for k in range(KC):
                        slot = part[("v", k // 4)][0]
                        te.matmul(pb[bank + hf][0:cw, 0:512],
                                  lnq[k][:, t * 128:t * 128 + cw],
                                  wqkv_slot[slot][:, (k % 4) * 1024 + hf * 512:
                                                  (k % 4) * 1024 + (hf + 1) * 512],
                                  start=(k == 0), stop=(k == KC - 1))
                te.drain()
                s_pe.inc(te.last)
            v_mm[t] = s_pe.n

            @block.vector
            def _(v, t=t, cw=cw, bank=bank):
                v = TRK(v)
                v.wait_ge(s_pe.h, v_mm[t])
                v.wait_ge(s_rb.h, bv_ready)
                if t < 2:
                    v.wait_ge(s_dv.h, ffn1_h_last)
                for hf in range(2):
                    v.scalar_tensor_tensor(v_tm[t][0:cw, hf * 512:(hf + 1) * 512],
                                           pb[bank + hf][0:cw, 0:512], 1.0,
                                           bias_bc[0:cw, hf * 512:(hf + 1) * 512],
                                           ALU.mult, ALU.add)
                v.drain()
                s_dv.inc(v.last)
            v_evac[t] = s_dv.n
        v_all = v_evac[NCH - 1]
        v_mm_last = v_mm[NCH - 1]

        stream_part("q", wq_cm, 0, 1, k_proj_done_pe)
        stream_part("q", wq_cm, 1, 2, v_mm_last)
        emit_proj("q", lnq, bq_t, v_all)
        q_evac_all = kq_evac[("q", NST - 1, 1)]
        q_mm_last = last_mm["q"]

        @block.gpsimd
        def _(g):
            g = TRK(g)
            g.wait_ge(s_pe.h, q_mm_last)
            s_wo.inc(g.dma_start(out=wo_bf[:, :], in_=wo_cm), 16)
            g.wait_ge(s_dv.h, v_all)
            s_rb.inc(g.dma_start(out=bias_bc[:, :],
                                 in_=bvo_d[1:2, :].partition_broadcast(128)), 16)
        wo_ready = s_wo.n
        bo_ready = s_rb.n

        @block.vector
        def _(v):
            v = TRK(v)
            v.wait_ge(s_pe.h, q_mm_last)
            for h in range(8):
                v.memset(p_buf[h][:, :], 0.0)
            v.drain()
            s_dv.inc(v.last)
        pz_done = s_dv.n

        A = {}
        import os
        attn_tmax = int(os.environ.get("ATTN_TMAX", NCH))
        attn_sub = int(os.environ.get("ATTN_SUB", 7))

        def mask_off(t):
            return 0 if t == 0 else (320 if t == NCH - 1 else 160)

        def emit_qk(t):
            cw = min(128, TLOC - t * 128)
            kw = 160 if t < NCH - 1 else 48
            for hh in range(2):
                heads = list(range(hh * 8, hh * 8 + 8))

                @block.tensor
                def _(te, t=t, cw=cw, hh=hh, heads=heads, kw=kw):
                    te = TRK(te)
                    te.wait_ge(s_ac.h, q_evac_all)
                    te.wait_ge(s_dv.h, v_all)
                    te.wait_ge(s_dv.h, pz_done)
                    te.wait_ge(s_c.h, consts_done)
                    te.wait_ge(s_cg.h, cg_done)
                    prev = A.get((t - 1, hh, "ptcopy"))
                    if prev:
                        te.wait_ge(s_dv.h, prev)
                    for grp in (0, 1):
                        for h in heads:
                            if h % 2 != grp:
                                continue
                            hl = h % 8
                            bank = 3 * hh + hl // 3
                            off = (hl % 3) * 160
                            jt = h // 2
                            pbase = 64 * grp
                            te.matmul(pb[bank][0:cw, off:off + kw],
                                      lnq[jt][pbase:pbase + 64, t * 128:t * 128 + cw],
                                      k_fm[jt][pbase:pbase + 64, t * 128:t * 128 + kw],
                                      start=True, stop=True)
                        te.drain()
                    s_pe.inc(te.last)
                A[(t, hh, "qk")] = s_pe.n

        def emit_front(t, halves):
            cw = min(128, TLOC - t * 128)
            kw = 160 if t < NCH - 1 else 48
            for hh in halves:
                heads = list(range(hh * 8, hh * 8 + 8))

                @block.vector
                def _(v, t=t, cw=cw, hh=hh, heads=heads, kw=kw):
                    v = TRK(v)
                    v.wait_ge(s_pe.h, A[(t, hh, "qk")])
                    for h in heads:
                        hl = h % 8
                        S = pb[3 * hh + hl // 3][0:cw,
                                                 (hl % 3) * 160:(hl % 3) * 160 + kw]
                        v.scalar_tensor_tensor(
                            S, S, 0.125,
                            mask_t[0:cw, mask_off(t):mask_off(t) + kw],
                            ALU.mult, ALU.add)
                    v.drain()
                    s_dv.inc(v.last)
                A[(t, hh, "stg")] = s_dv.n

                @block.scalar
                def _(sc, t=t, cw=cw, hh=hh, heads=heads, kw=kw):
                    sc = TRK(sc)
                    sc.wait_ge(s_dv.h, A[(t, hh, "stg")])
                    ptprev = A.get((t, 0, "pt")) if hh == 1 else \
                        A.get((t - 1, 1, "pt"))
                    if ptprev:
                        sc.wait_ge(s_pe.h, ptprev)
                    for h in heads:
                        hl = h % 8
                        S = pb[3 * hh + hl // 3][0:cw,
                                                 (hl % 3) * 160:(hl % 3) * 160 + kw]
                        sc.activation(p_buf[hl][0:cw, 112:112 + kw], S,
                                      AF.Exp)
                    sc.drain()
                    s_ac.inc(sc.last)
                A[(t, hh, "exp_all")] = s_ac.n

        pend_wo_evac = [None]
        if attn_tmax > 0:
            emit_qk(0)
            emit_front(0, (0,))
        for t in range(attn_tmax):
            cw = min(128, TLOC - t * 128)
            pg = t % 2
            nm = astat[:, pg * 48:pg * 48 + 16]
            Zs = astat[:, pg * 48 + 16:pg * 48 + 32]
            rr = astat[:, pg * 48 + 32:pg * 48 + 48]
            pieces = [c for c in (t - 1, t, t + 1) if 0 <= c <= NCH - 1]

            kw = 160 if t < NCH - 1 else 48
            for hh in range(2):
                heads = list(range(hh * 8, hh * 8 + 8))
                if hh == 1:
                    emit_front(t, (1,))

                if attn_sub < 3:
                    A[(t, hh, "scaled")] = s_dv.n
                    A[(t, hh, "pt")] = s_pe.n
                    A[(t, hh, "ptcopy")] = s_dv.n
                    A[(t, hh, "pv")] = s_pe.n
                    A[(t, hh, "vals")] = s_dv.n
                    continue

                @block.vector
                def _(v, t=t, cw=cw, hh=hh, heads=heads, Zs=Zs, rr=rr, kw=kw):
                    v = TRK(v)
                    v.wait_ge(s_ac.h, A[(t, hh, "exp_all")])
                    for h in heads:
                        v.reduce_sum(Zs[0:cw, h:h + 1],
                                     p_buf[h % 8][0:cw, 112:112 + kw], axis=AX.X)
                    v.drain()
                    v.reciprocal(rr[0:cw, hh * 8:hh * 8 + 8],
                                 Zs[0:cw, hh * 8:hh * 8 + 8])
                    v.drain()
                    for h in heads:
                        v.tensor_scalar_mul(p_buf[h % 8][0:cw, 112:112 + kw],
                                            p_buf[h % 8][0:cw, 112:112 + kw],
                                            rr[0:cw, h:h + 1])
                    v.drain()
                    s_dv.inc(v.last)
                A[(t, hh, "scaled")] = s_dv.n
                if hh == 0 and pend_wo_evac[0] is not None:
                    pend_wo_evac[0]()
                    pend_wo_evac[0] = None
                if attn_sub < 4:
                    A[(t, hh, "pt")] = s_pe.n
                    A[(t, hh, "ptcopy")] = s_dv.n
                    A[(t, hh, "pv")] = s_pe.n
                    A[(t, hh, "vals")] = s_dv.n
                    continue

                @block.tensor
                def _(te, t=t, hh=hh, heads=heads, pieces=pieces):
                    te = TRK(te)
                    te.wait_ge(s_dv.h, A[(t, hh, "scaled")])
                    slot = 0
                    for h in heads:
                        for c in pieces:
                            ci = c - (t - 1)
                            bank = 3 * hh + slot // 8
                            te.transpose(pbb[bank][:, (slot % 8) * 128:
                                                   (slot % 8 + 1) * 128],
                                         p_buf[h % 8][:, ci * 128:(ci + 1) * 128],
                                         eye_b[:, :])
                            slot += 1
                    te.drain()
                    s_pe.inc(te.last)
                A[(t, hh, "pt")] = s_pe.n
                if attn_sub < 5:
                    A[(t, hh, "ptcopy")] = s_dv.n
                    A[(t, hh, "pv")] = s_pe.n
                    A[(t, hh, "vals")] = s_dv.n
                    continue

                @block.vector
                def _(v, t=t, hh=hh, heads=heads, pieces=pieces):
                    v = TRK(v)
                    v.wait_ge(s_pe.h, A[(t, hh, "pt")])
                    ppv = A.get((t, 0, "pv")) if hh == 1 else A.get((t - 1, 1, "pv"))
                    if ppv:
                        v.wait_ge(s_pe.h, ppv)
                    nslots = len(pieces) * 8
                    for b in range((nslots + 7) // 8):
                        w = min(8, nslots - b * 8)
                        v.tensor_copy(pt_sb[:, b * 1024:b * 1024 + w * 128],
                                      pbb[3 * hh + b][:, 0:w * 128])
                    v.drain()
                    s_dv.inc(v.last)
                A[(t, hh, "ptcopy")] = s_dv.n
                if attn_sub < 6:
                    A[(t, hh, "pv")] = s_pe.n
                    A[(t, hh, "vals")] = s_dv.n
                    continue

                @block.tensor
                def _(te, t=t, cw=cw, hh=hh, heads=heads, pieces=pieces):
                    te = TRK(te)
                    te.wait_ge(s_dv.h, A[(t, hh, "ptcopy")])
                    prev = A.get((t, 0, "vals")) if hh == 1 else \
                        A.get((t - 1, "wo_evac"))
                    if prev:
                        te.wait_ge(s_dv.h, prev)
                    slot = 0
                    for h in heads:
                        hl = h % 8
                        bank = 6 + hh
                        off = (hl // 2) * 128
                        pbase = 64 * (hl % 2)
                        for idx, c in enumerate(pieces):
                            kcnt = min(128, TLOC - c * 128)
                            te.matmul(pb[bank][pbase:pbase + 64, off:off + 128],
                                      v_tm[c][0:kcnt, h * 64:(h + 1) * 64],
                                      pt_sb[0:kcnt, slot * 128:(slot + 1) * 128],
                                      start=(idx == 0), stop=(idx == len(pieces) - 1))
                            slot += 1
                    te.drain()
                    s_pe.inc(te.last)
                A[(t, hh, "pv")] = s_pe.n

                @block.vector
                def _(v, t=t, hh=hh, heads=heads, pg=pg):
                    v = TRK(v)
                    v.wait_ge(s_pe.h, A[(t, hh, "pv")])
                    wo_prev = A.get((t - 1, "wo"))
                    if wo_prev:
                        v.wait_ge(s_pe.h, wo_prev)
                    for jl in range(4):
                        jt = 4 * hh + jl
                        v.tensor_copy(vals[:, jt * 128:jt * 128 + 128],
                                      pb[6 + hh][:, jl * 128:jl * 128 + 128])
                    v.drain()
                    s_dv.inc(v.last)
                A[(t, hh, "vals")] = s_dv.n

            if attn_sub < 7:
                A[(t, "wo")] = s_pe.n
                A[(t, "wo_evac")] = s_dv.n
                continue

            if t + 1 < attn_tmax:
                emit_qk(t + 1)
                emit_front(t + 1, (0,))

            @block.tensor
            def _(te, t=t, cw=cw, pg=pg):
                te = TRK(te)
                te.wait_ge(s_dv.h, A[(t, 1, "vals")])
                te.wait_ge(s_wo.h, wo_ready)
                for hf in range(2):
                    for j in range(KC):
                        te.matmul(pb[6 + hf][0:cw, 0:512],
                                  vals[:, j * 128:j * 128 + cw],
                                  wo_bf[:, j * 1024 + hf * 512:j * 1024 + (hf + 1) * 512],
                                  start=(j == 0), stop=(j == KC - 1))
                te.drain()
                s_pe.inc(te.last)
            A[(t, "wo")] = s_pe.n

            def emit_wo_evac(t=t, cw=cw):
                @block.vector
                def _(v, t=t, cw=cw):
                    v = TRK(v)
                    v.wait_ge(s_pe.h, A[(t, "wo")])
                    v.wait_ge(s_rb.h, bo_ready)
                    for hf in range(2):
                        v.scalar_tensor_tensor(y[t][0:cw, hf * 512:(hf + 1) * 512],
                                               pb[6 + hf][0:cw, 0:512], 1.0,
                                               y[t][0:cw, hf * 512:(hf + 1) * 512],
                                               ALU.mult, ALU.add)
                    v.drain()
                    v.tensor_tensor(y[t][0:cw, :], y[t][0:cw, :], bias_bc[0:cw, :],
                                    ALU.add)
                    v.drain()
                    s_dv.inc(v.last)
                A[(t, "wo_evac")] = s_dv.n
            pend_wo_evac[0] = emit_wo_evac
            if t == attn_tmax - 1:
                emit_wo_evac()
                pend_wo_evac[0] = None

        if attn_tmax > 0:
            z_done = {t: A[(min(t, attn_tmax - 1), "wo_evac")] for t in range(NCH)}
            attn_qk_last = A[(attn_tmax - 1, 1, "qk")]
            attn_pv_last = A[(attn_tmax - 1, 1, "pv")]
            attn_wo_last = A[(attn_tmax - 1, "wo")]
        else:
            z_done = {t: pz_done for t in range(NCH)}
            attn_qk_last = attn_pv_last = attn_wo_last = s_pe.n
    else:
        z_done = y1_done
        attn_qk_last = attn_pv_last = attn_wo_last = 0

    if "z" in tapd and stage in ("attn", "full"):
        @block.sync
        def _(sync):
            sync = TRK(sync)
            for t in range(NCH):
                sync.wait_ge(s_dv.h, z_done[t])
                s_st.inc(sync.dma_start(out=tapd["z"][t * 128:(t + 1) * 128, :],
                                        in_=y[t][:, :]), 16)

    if stage == "full":
        ln3_done = emit_layernorm(y, lnq,
                                  src_gate=lambda t: ((s_dv, z_done[t]),),
                                  dest_gate=attn_qk_last,
                                  bank_gate=z_done[NCH - 1],
                                  sq_pe_gate=attn_pv_last)
        lg2_last, chain2 = emit_router(
            ln3_done, z_done[NCH - 1], entry_pe=attn_wo_last)
        yfinal, _g, _d, _h = emit_ffn(1, ln3_done[NCH - 1],
                                      entry_pe=attn_wo_last,
                                      entry_dv0=z_done[NCH - 1],
                                      entry_dv1=max(z_done[NCH - 1], lg2_last),
                                      chain_fn=chain2)
    elif stage == "attn":
        yfinal = z_done
    else:
        yfinal = y1_done

    @block.sync
    def _(sync):
        sync = TRK(sync)
        for t in range(NCH):
            sync.wait_ge(s_dv.h, yfinal[t])
            if t == 0:
                s_st.inc(sync.dma_start(out=y_out[0:112, :], in_=y[0][16:128, :]), 16)
            elif t < 16:
                s_st.inc(sync.dma_start(out=y_out[112 + (t - 1) * 128:112 + t * 128, :],
                                        in_=y[t][:, :]), 16)
            else:
                s_st.inc(sync.dma_start(out=y_out[2032:2048, :], in_=y[16][0:16, :]), 16)
        sync.wait_ge(s_st.h, s_st.n)

    finals = {sem.h: sem.n for sem in
              (s_x, s_c, s_cg, s_pe, s_dv, s_ac, s_st, s_wd, s_wo, s_rb,
               *s_wg, *s_wq)}
    for h, val, stack in _WAITS:
        if h in finals and isinstance(val, int) and val > finals[h]:
            raise AssertionError(
                f"unreachable wait: sem {h.name} waited {val} > final {finals[h]}"
                f"  at {stack}")
    _WAITS.clear()

    es.close()
    return nc, tapd


def _halves(w):
    return np.ascontiguousarray(
        np.concatenate([w[:, 0:4096], w[:, 4096:8192]], axis=0))


def _prep_shared(inp):
    f32 = np.float32
    Wqkv = np.asarray(inp["L0_attn_Wqkv"], f32)
    bqkv = np.asarray(inp["L0_attn_bqkv"], f32)
    Wo = np.asarray(inp["L0_attn_Wo"], f32)
    bo = np.asarray(inp["L0_attn_bo"], f32)
    rw = np.asarray(inp["L0_ffn_router_w"], f32)
    wg_r = np.asarray(inp["L0_ffn_wg_r"], f32)
    wu_r = np.asarray(inp["L0_ffn_wu_r"], f32)
    wd_r = np.asarray(inp["L0_ffn_wd_r"], f32)
    wg_s = np.asarray(inp["L0_ffn_wg_s"], f32)
    wu_s = np.asarray(inp["L0_ffn_wu_s"], f32)
    wd_s = np.asarray(inp["L0_ffn_wd_s"], f32)

    f = np.arange(3 * D)
    r = f % (3 * HD)
    q_cols = f[r < HD]
    k_cols = f[(r >= HD) & (r < 2 * HD)]
    v_cols = f[r >= 2 * HD]
    Wq = Wqkv[:, q_cols]
    Wk = Wqkv[:, k_cols]
    Wv = Wqkv[:, v_cols]
    bq = bqkv[q_cols]
    bk = bqkv[k_cols]
    bv = bqkv[v_cols]

    wg_pe, wu_pe, wd_pe = [], [], []
    for n in range(2):
        for hf in range(2):
            wg_pe.append(wg_s[n][:, hf * 256:(hf + 1) * 256])
            wu_pe.append(wu_s[n][:, hf * 256:(hf + 1) * 256])
            wd_pe.append(wd_s[n][hf * 256:(hf + 1) * 256, :])
    for e in range(16):
        wg_pe.append(wg_r[e]); wu_pe.append(wu_r[e]); wd_pe.append(wd_r[e])

    wg_cat = np.concatenate([cm_layout(m) for m in wg_pe], axis=0)
    wu_cat = np.concatenate([cm_layout(m) for m in wu_pe], axis=0)
    wd_cat = np.concatenate(
        [cm_layout(np.concatenate([wd_pe[G * 4 + i] for i in range(4)], axis=0))
         for G in range(NG)], axis=0)

    return {
        "wg_cat": wg_cat, "wu_cat": wu_cat, "wd_cat": wd_cat,
        "rw_cm": cm_layout(rw),
        "wq_cm": _halves(cm_layout(Wq)), "wk_cm": _halves(cm_layout(Wk)),
        "wv_cm": _halves(cm_layout(Wv)), "wo_cm": cm_layout(Wo),
        "bq_pp": np.ascontiguousarray(bq.reshape(8, 128).T),
        "bk_pp": np.ascontiguousarray(bk.reshape(8, 128).T),
        "bvo_d": np.ascontiguousarray(np.stack([bv, bo])),
        "eyes_d": np.eye(128, dtype=f32),
    }


def _band_masks(parity):
    i = np.arange(128)[:, None]
    j = np.arange(160)[None, :]
    band = (j >= i) & (j <= i + 32)
    m_first = band.copy()
    m_mid = band.copy()
    m_last = band.copy()
    if parity == 0:
        m_first &= (j >= 32)
    else:
        m_last &= (j <= 31)
    packed = np.empty((128, 480), BFDT)
    for a, m in enumerate((m_first, m_mid, m_last)):
        mm = np.where(m, np.float32(0.0), np.float32(NEG))
        bad = ~m.any(axis=1)
        mm[bad, 0] = 0.0
        packed[:, a * 160:(a + 1) * 160] = mm.astype(BFDT)
    return packed


def kernel(**inputs):
    x = np.asarray(inputs["L0_attn_x"], np.float32)
    B, S, _ = x.shape
    shared = _prep_shared(inputs)

    in_maps = []
    for c in range(8):
        b, half = c // 2, c % 2
        qs = half * 2048
        xl = np.zeros((TLOC, D), np.float32)
        lo, hi = qs - 16, qs + 2064
        slo, shi = max(lo, 0), min(hi, S)
        xl[slo - lo:shi - lo, :] = x[b, slo:shi, :]
        m = dict(shared)
        m["x_l"] = xl
        m["masks_d"] = _band_masks(half)
        in_maps.append(m)

    nc, _ = build_program()
    res = run_bass_kernel_spmd(nc, in_maps, core_ids=list(range(8)))
    out = np.empty((B, S, D), np.float32)
    for c in range(8):
        b, half = c // 2, c % 2
        out[b, half * 2048:(half + 1) * 2048, :] = res.results[c]["y_out"]
    return out
